# version 10
# speedup vs baseline: 1.5127x; 1.0101x over previous
from contextlib import ExitStack
import numpy as np
import ml_dtypes

import concourse.bass as bass
import concourse.mybir as mybir
from concourse.bass_utils import run_bass_kernel_spmd

F32 = mybir.dt.float32
BF16 = mybir.dt.bfloat16
I8 = mybir.dt.int8
FP8 = mybir.dt.float8e4
AF = mybir.ActivationFunctionType
ALU = mybir.AluOpType
AXX = mybir.AxisListType.X
BF = ml_dtypes.bfloat16
AP = bass.AP

D = 2048
T = 2048
TOWN = 1024
HD = 64
HID = 5632
NCHUNK = 16
OCHUNK = 8
DCH = 16
HCH = 44

ENGINES = ("sync", "tensor", "vector", "scalar", "gpsimd")


class Sched:

    def __init__(self, nc):
        self.nc = nc
        self.q = {e: [] for e in ENGINES}
        self.h = {}
        self.cnt = {}
        self.waited = {e: {} for e in ENGINES}
        self.ctx = ExitStack()

    def sem(self, name):
        self.h[name] = self.ctx.enter_context(self.nc.semaphore(name))
        self.cnt[name] = 0

    AUTOSEM = {"vector": "vp", "scalar": "sp"}

    def do(self, eng, fn, waits=(), inc=None, inc_n=1):
        auto = self.AUTOSEM.get(eng)
        waits = [wt for wt in waits if wt is not None]
        if auto is not None:
            waits.append((auto, self.cnt[auto]))
            assert inc in (None, auto), (eng, inc)
            inc, inc_n = auto, 1
        for s, v in waits:
            assert v <= self.cnt[s], (eng, s, v, self.cnt[s], "wait on future")
        w = []
        for s, v in waits:
            if v > self.waited[eng].get(s, 0):
                w.append((self.h[s], v))
                self.waited[eng][s] = v
        i = None
        if inc is not None:
            self.cnt[inc] += inc_n
            i = (self.h[inc], inc_n)
        self.q[eng].append((fn, w, i))
        return (inc, self.cnt[inc]) if inc is not None else None

    def dma(self, eng, out, in_, waits=(), q="qs"):
        return self.do(eng, lambda e, o=out, i=in_: e.dma_start(o, i),
                       waits=waits, inc=q, inc_n=16)

    def now(self, sem):
        return (sem, self.cnt[sem])

    def bnd(self):
        return ([self.now("vp"), self.now("tp"), self.now("sp")] +
                [self.now(s) for s in self.cnt if s.startswith("qg")])

    def finalize(self, block):
        for ename in ENGINES:
            ops = self.q[ename]

            def body(eng, ops=ops):
                for fn, w, i in ops:
                    for hh, v in w:
                        eng.wait_ge(hh, v)
                    inst = fn(eng)
                    if i is not None:
                        inst.then_inc(i[0], i[1])
            getattr(block, ename)(body)


class Rot:
    def __init__(self, n):
        self.t = [None] * n
        self.n = n

    def get(self, i):
        return self.t[i % self.n]

    def set(self, i, tok):
        self.t[i % self.n] = tok


def build(taps=()):
    nc = bass.Bass("TRN2", target_bir_lowering=False,
                   detect_race_conditions=False)

    EI = lambda n, s, dt: nc.dram_tensor(n, s, dt, kind="ExternalInput")
    x_in = EI("x", [T, D], F32)
    x_own = EI("x_own", [TOWN, D], F32)
    l0wq = EI("l0wq", [D, 1024], F32)
    l0wk = EI("l0wk", [D, 256], F32)
    l0wv = EI("l0wv", [D, 256], F32)
    l0wo = EI("l0wo", [D, D], F32)
    l0w8 = EI("l0w8", [D, D], FP8)
    w0sI = EI("w0s", [128, 1], F32)
    in_sI = EI("in_s", [128, 1], BF16)
    cosI = EI("cosP", [128, 512], BF16)
    sinI = EI("sinP", [128, 512], BF16)
    identI = EI("identI", [128, 128], BF16)
    tri01I = EI("tri01", [128, 128], BF16)
    m1I = EI("m1", [128, 128], BF16)
    wmI = EI("wmask", [128, 2048], BF16)
    w1mI = EI("w1mask", [128, 2048], BF16)
    zAI = EI("zA", [128, 1], F32)
    aliveI = EI("alive", [128, 16], F32)
    halfI = EI("halfsel", [128, 2], F32)
    wqtI = EI("wqt", [D + 1, D], F32)
    wktI = EI("wkt", [D + 1, D], F32)
    wvtI = EI("wvt", [D + 1, D], F32)
    wotI = EI("wot", [D + 1, D], F32)
    w1tI = EI("w1t", [D, HID], F32)
    w3tI = EI("w3t", [D, HID], F32)
    w2tI = EI("w2t", [HID, D], F32)
    abs1I = EI("abs1", [256, HID], F32)
    abs3I = EI("abs3", [256, HID], F32)
    abs2I = EI("abs2", [704, D], F32)

    out = nc.dram_tensor("out", [TOWN, D], F32, kind="ExternalOutput")

    h_scr = nc.dram_tensor("h_scr", [TOWN, HID], F32)
    x1_scr = nc.dram_tensor("x1_scr", [TOWN, D], F32)
    ag_in = nc.dram_tensor("ag_in", [TOWN, 2 * D], BF16)
    ag_out = nc.dram_tensor("ag_out", [T, 2 * D], BF16)
    ar_in = nc.dram_tensor("ar_in", [1, 4], F32)
    ar_out = nc.dram_tensor("ar_out", [1, 4], F32)

    QG_ALL = ("qg", "qg0", "qg1", "qg2", "qg3", "qg4", "qg5", "qg6", "qg7",
              "qg8", "qg9")
    S = Sched(nc)
    for s in ("qs", "qs0", "qs1", "qs2", "qs3", "qs4", "qs5", "qs6", "qs7",
              "tp", "vp", "sp", "cc") + QG_ALL:
        S.sem(s)
    ctx = S.ctx
    sbent = lambda n, s, dt: ctx.enter_context(nc.sbuf_tensor(n, s, dt))

    class Arena:

        def __init__(self, base, size):
            self.base, self.size = base, size
            self.freelist = [(0, size)]
            self.amap = {}
            self.peak = 0

        def alloc(self, name, shape, dt):
            per = mybir.dt.size(dt)
            for sdim in shape[1:]:
                per *= sdim
            per = (per + 31) // 32 * 32
            for i, (o, s) in enumerate(self.freelist):
                if s >= per:
                    self.freelist[i] = (o + per, s - per)
                    self.amap[name] = (o, per)
                    self.peak = max(self.peak, o + per)
                    return nc.alloc_sbuf_tensor_at(name, shape, dt,
                                                   offset=self.base + o)
            raise MemoryError((name, per, self.freelist))

        def release(self, *names):
            for n in names:
                o, s = self.amap.pop(n)
                self.freelist.append((o, s))
            self.freelist.sort()
            merged = []
            for o, s in self.freelist:
                if merged and merged[-1][0] + merged[-1][1] == o:
                    merged[-1] = (merged[-1][0], merged[-1][1] + s)
                else:
                    merged.append((o, s))
            self.freelist = merged

    arena_ref = []

    def sbuf(name, shape, dt):
        return arena_ref[0].alloc(name, shape, dt)

    def sbuf_free(*names):
        arena_ref[0].release(*names)

    ident = sbent("identc", [128, 128], BF16)
    tri01 = sbent("tri01c", [128, 128], BF16)
    m1 = sbent("m1c", [128, 128], BF16)
    wmk = sbent("wmk", [128, 2048], BF16)
    w1mk = sbent("w1mk", [128, 2048], BF16)
    zA = sbent("zAc", [128, 1], F32)
    alive = sbent("alivec", [128, 16], F32)
    cosT = sbent("cosTc", [128, 16, 32], BF16)
    sinT = sbent("sinTc", [128, 16, 32], BF16)
    halfc = sbent("halfc", [128, 2], F32)
    onesbf = sbent("onesbf", [1, 128], BF16)
    onesf = sbent("onesf", [1, 128], F32)
    onescf = sbent("onescf", [128, 1], F32)
    eighth = sbent("eighth", [128, 1], F32)
    epsT = sbent("epsT", [128, 1], F32)
    sc_misc = sbent("sc_misc", [128, 8], F32)
    mu128 = sbent("mu128", [128, 8], F32)
    qn2acc = sbent("qn2acc", [128, 32], F32)
    kn2acc = sbent("kn2acc", [128, 32], F32)
    cterm0 = sbent("cterm0", [128, 16], F32)
    cterm1 = sbent("cterm1", [128, 32], F32)
    statT = sbent("statT", [128, 2, 40], F32)
    rall = sbent("rall", [128, 8], F32)
    sc1a = sbent("sc1a", [128, 8], F32)
    sc3a = sbent("sc3a", [128, 8], F32)
    am2 = sbent("am2", [128, 8], F32)
    scq2 = sbent("scq2", [128, 8], F32)
    scg2 = sbent("scg2", [128, 8], F32)
    rowscr = sbent("rowscr", [1, 64], F32)
    rowbf = sbent("rowbf", [1, 64], BF16)
    colscr = sbent("colscr", [128, 8], F32)
    colbf = sbent("colbf", [128, 32], BF16)
    sbf16 = sbent("sbf16", [128, 1], BF16)

    psF = ctx.enter_context(nc.psum_tensor("psF", [128, 3072], F32))
    psB16cm = nc.psum_tensor("psB16", [128, 2048], BF16)
    psB16 = psB16cm.__enter__()

    _abase = (nc._sbuf_addr_for_side("left") + 31) // 32 * 32
    _asize = (nc.sbuf_bytes_remaining - 64) // 32 * 32
    arena_ref.append(Arena(_abase, _asize))

    TAPS = []

    def tap(name, hh):
        if name in taps:
            flat = 1
            for sdim in hh.shape[1:]:
                flat *= sdim
            tdr = nc.dram_tensor("tap_" + name, [128, flat], hh.dtype,
                                 kind="ExternalOutput")
            S.dma("gpsimd", tdr[:], AP(hh, 0, [[flat, 128], [1, flat]]),
                  waits=S.bnd(), q="qg")

    def bar(name):
        if name in taps:
            S.do("gpsimd", lambda e: e.nop(), waits=S.bnd())

    absacc = sbuf("absacc", [128, 4], F32)
    bigbuf = sbuf("bigbuf", [128, 2, 2816], F32)

    S.dma("sync", ident[:], identI[:])
    S.dma("sync", tri01[:], tri01I[:])
    S.dma("sync", m1[:], m1I[:])
    S.dma("sync", wmk[:], wmI[:])
    S.dma("sync", w1mk[:], w1mI[:])
    S.dma("sync", zA[:], zAI[:])
    S.dma("sync", alive[:], aliveI[:])
    S.dma("sync", halfc[:], halfI[:])
    S.dma("sync", cosT[:], cosI.ap().rearrange("p (c k) -> p c k", k=32))
    S.dma("sync", sinT[:], sinI.ap().rearrange("p (c k) -> p c k", k=32))
    S.dma("sync", sc_misc[:, 2:3], w0sI[:])
    t_con = S.dma("sync", sbf16[:], in_sI[:])

    S.do("vector", lambda e: e.memset(onesbf[:], 1.0))
    S.do("vector", lambda e: e.memset(onesf[:], 1.0))
    S.do("vector", lambda e: e.memset(onescf[:], 1.0))
    S.do("vector", lambda e: e.memset(eighth[:], 0.125))
    S.do("vector", lambda e: e.memset(epsT[:], 1e-5))
    S.do("vector", lambda e: e.memset(absacc[:], 0.0))
    S.do("vector", lambda e: e.tensor_copy(colscr[:, 0:1], sbf16[:]), waits=[t_con])
    S.do("vector", lambda e: e.reciprocal(sc_misc[:, 0:1], colscr[:, 0:1]))
    v_consts = S.do("vector", lambda e: e.tensor_tensor(
        sc_misc[:, 1:2], colscr[:, 0:1], sc_misc[:, 2:3], ALU.mult), inc="vp")

    bigrot = Rot(2)

    def abs_accum(src, nrows, ncols, accol, idx0):
        idx = idx0
        nch = (nrows + 127) // 128
        nj = (ncols + 2815) // 2816
        for c in range(nch):
            pr = min(128, nrows - c * 128)
            for j in range(nj):
                w = min(2816, ncols - j * 2816)
                tok = S.dma("sync", bigbuf[0:pr, idx % 2, 0:w],
                            src[c * 128:c * 128 + pr, j * 2816:j * 2816 + w],
                            waits=[bigrot.get(idx), v_consts], q=f"qs{idx % 2}")
                S.do("vector", lambda e, pr=pr, b=idx % 2, w=w: e.reduce_sum(
                    colscr[0:pr, 4:5], bigbuf[0:pr, b, 0:w], axis=AXX,
                    apply_absolute_value=True), waits=[tok])
                bigrot.set(idx, S.do("vector", lambda e, pr=pr, a=accol: e.tensor_tensor(
                    absacc[0:pr, a:a + 1], absacc[0:pr, a:a + 1],
                    colscr[0:pr, 4:5], ALU.add), inc="vp"))
                idx += 1
        return idx

    nidx = abs_accum(abs1I, 256, HID, 0, 0)
    nidx = abs_accum(abs3I, 256, HID, 1, nidx)
    nidx = abs_accum(abs2I, 704, D, 2, nidx)
    t_sum = S.do("tensor", lambda e: e.matmul(psF[0:1, 0:4], onescf[:], absacc[:],
                                              start=True, stop=True),
                 waits=[S.now("vp")], inc="tp")
    v_sum = S.do("vector", lambda e: e.tensor_copy(colscr[0:1, 0:4], psF[0:1, 0:4]),
                 waits=[t_sum], inc="vp")
    g_ar = S.dma("gpsimd", ar_in[:], colscr[0:1, 0:4], waits=[v_sum], q="qg0")
    cc_mu = S.do("gpsimd", lambda e: e.collective_compute(
        "AllReduce", ALU.add, replica_groups=[list(range(8))],
        ins=[ar_in.ap().opt()], outs=[ar_out.ap().opt()]),
        waits=[g_ar], inc="cc")
    t_muld = S.dma("sync", rowscr[0:1, 0:4], ar_out[:], waits=[cc_mu])
    S.do("vector", lambda e: e.tensor_scalar(rowscr[0:1, 4:6], rowscr[0:1, 0:2],
                                             1.0 / (HID * D), 1e-5, ALU.mult, ALU.max),
         waits=[t_muld])
    S.do("vector", lambda e: e.tensor_scalar(rowscr[0:1, 6:7], rowscr[0:1, 2:3],
                                             1.0 / (HID * D), 1e-5, ALU.mult, ALU.max))
    v_mu = S.do("vector", lambda e: e.reciprocal(rowscr[0:1, 8:11], rowscr[0:1, 4:7]),
                inc="vp")
    S.do("tensor", lambda e: e.matmul(psF[:, 4:7], onesf[:], rowscr[0:1, 8:11],
                                      start=True, stop=True), waits=[v_mu])
    t_mu2 = S.do("tensor", lambda e: e.matmul(psF[:, 8:11], onesf[:], rowscr[0:1, 4:7],
                                              start=True, stop=True), inc="tp")
    S.do("vector", lambda e: e.tensor_copy(mu128[:, 0:3], psF[:, 4:7]), waits=[t_mu2])
    v_mu128 = S.do("vector", lambda e: e.tensor_copy(mu128[:, 4:7], psF[:, 8:11]),
                   inc="vp")

    TRC = [0]
    psrot = Rot(2)

    def transpose_tile(srcf, dstf, nblk, wait_first):
        lastev = lasttp = None
        nhg = (nblk + 7) // 8
        for hg in range(nhg):
            nb = min(8, nblk - hg * 8)
            key = TRC[0]
            TRC[0] += 1
            slot = key % 2
            wl = [wait_first, psrot.get(key)]
            tptok = None
            for c in range(nb):
                cc = hg * 8 + c
                tptok = S.do("tensor", lambda e, cc=cc, c=c, slot=slot, nb=nb:
                             e.matmul(
                                 psB16[:, slot * 1024 + c * 128:slot * 1024 + (c + 1) * 128],
                                 srcf(cc), ident[:], start=(c == 0),
                                 stop=(c == nb - 1), is_transpose=True),
                             waits=wl if c == 0 else (),
                             inc="tp" if c == nb - 1 else None)
            ev = S.do("vector", lambda e, hg=hg, nb=nb, slot=slot: e.tensor_copy(
                dstf(hg, nb),
                AP(psB16, slot * 1024, [[2048, 128], [128, nb], [1, 128]])),
                waits=[tptok], inc="vp")
            psrot.set(key, ev)
            lastev, lasttp = ev, tptok
        return lastev, lasttp

    ln_guard = Rot(2)

    def ln_stats(src_ap, stat_wait, slot):
        st = statT[:, slot, :]
        for c in range(4):
            S.do("vector", lambda e, s=src_ap, st=st, c=c: e.bn_stats(
                st[:, 6 * c:6 * c + 6], s[:, 512 * c:512 * (c + 1)]),
                waits=(stat_wait + [ln_guard.get(slot)]) if c == 0 else ())
        v_ag = S.do("vector", lambda e, st=st: e.bn_aggr(st[:, 24:26],
                                                         st[:, 0:24]),
                    inc="vp")
        s_sd = S.do("scalar", lambda e, st=st: e.activation(
            st[:, 26:27], st[:, 25:26], AF.Sqrt, bias=epsT[:]),
            waits=[v_ag], inc="sp")
        S.do("vector", lambda e, st=st: e.reciprocal(st[:, 27:28], st[:, 26:27]),
             waits=[s_sd])
        return S.do("vector", lambda e, st=st: e.scalar_tensor_tensor(
            st[:, 28:29], st[:, 24:25], -1.0, st[:, 27:28], ALU.mult, ALU.mult),
            inc="vp")

    def cterm_pipe(qt, qc0, kap, dst, dc0, nh, vguard):
        v0 = S.do("vector", lambda e, nh=nh: e.tensor_copy(
            colbf[:, 0:nh], qt[:, qc0:qc0 + nh]), waits=vguard, inc="vp")
        t0 = S.do("tensor", lambda e, nh=nh: e.transpose(
            psB16[0:nh, 0:128], colbf[:, 0:nh], ident[:]),
            waits=[v0, psrot.get(TRC[0])], inc="tp")
        v1 = S.do("vector", lambda e, nh=nh: e.reduce_max(
            colscr[0:nh, 0:1], psB16[0:nh, 0:128], axis=AXX), waits=[t0], inc="vp")
        v2 = S.do("vector", lambda e, nh=nh, kap=kap: e.tensor_copy(
            colbf[:, 0:nh], kap), waits=[v1], inc="vp")
        t1 = S.do("tensor", lambda e, nh=nh: e.transpose(
            psB16[0:nh, 0:128], colbf[:, 0:nh], ident[:]), waits=[v2], inc="tp")
        S.do("vector", lambda e, nh=nh: e.reduce_max(
            colscr[0:nh, 1:2], psB16[0:nh, 0:128], axis=AXX), waits=[t1])
        v3 = S.do("vector", lambda e, nh=nh: e.tensor_tensor(
            colscr[0:nh, 2:3], colscr[0:nh, 0:1], colscr[0:nh, 1:2], ALU.mult),
            inc="vp")
        s0 = S.do("scalar", lambda e, nh=nh: e.activation(
            colscr[0:nh, 3:4], colscr[0:nh, 2:3], AF.Sqrt), waits=[v3], inc="sp")
        S.do("vector", lambda e, nh=nh: e.tensor_scalar(
            colscr[0:nh, 4:5], colscr[0:nh, 3:4], -0.125, None, ALU.mult),
            waits=[s0])
        v4 = S.do("vector", lambda e, nh=nh: e.tensor_copy(
            colbf[0:nh, 16:17], colscr[0:nh, 4:5]), inc="vp")
        t2 = S.do("tensor", lambda e, nh=nh: e.transpose(
            psB16[0:1, 0:nh], colbf[0:nh, 16:17], ident[0:nh, 0:nh]),
            waits=[v4], inc="tp")
        v5 = S.do("vector", lambda e, nh=nh: e.tensor_copy(
            rowbf[0:1, 0:nh], psB16[0:1, 0:nh]), waits=[t2], inc="vp")
        t3 = S.do("tensor", lambda e, nh=nh: e.matmul(
            psF[:, 2560:2560 + nh], onesbf[:], rowbf[0:1, 0:nh], start=True,
            stop=True), waits=[v5], inc="tp")
        return S.do("vector", lambda e, nh=nh, dc0=dc0, dst=dst: e.tensor_copy(
            dst[:, dc0:dc0 + nh], psF[:, 2560:2560 + nh]), waits=[t3], inc="vp")

    def attention(nheads, nqw, nkc, lhsK, rhsQ, rhsV, biasAP, mask_list, evac_ap,
                  first_wait, PtT, rcpT):
        sc_rot = Rot(2)
        pt_rot = Rot(2)
        pc_rot = Rot(2)
        items = []
        for h in range(nheads):
            for qw in range(nqw):
                nk = nkc(qw)
                ng = (nk + 1) // 2
                for g in range(ng):
                    items.append((h, qw, g, nk, ng))

        def kcs_of(it):
            h, qw, g, nk, ng = it
            return [k for k in (2 * g, 2 * g + 1) if k < nk]

        def emit_scores(gidx, it):
            h, qw, g, nk, ng = it
            kcs = kcs_of(it)
            sc_off = 1024 * (gidx % 2)
            wl = [sc_rot.get(gidx)] + (list(first_wait) if gidx == 0 else [])
            ttok = None
            for ii, kc in enumerate(kcs):
                ttok = S.do("tensor", lambda e, ii=ii, kc=kc, sc_off=sc_off,
                            h=h, qw=qw: e.matmul(
                                psF[:, sc_off + ii * 512:sc_off + (ii + 1) * 512],
                                lhsK(h, kc), rhsQ(h, qw), start=True,
                                stop=True),
                            waits=wl if ii == 0 else (),
                            inc="tp" if ii == len(kcs) - 1 else None)
            return ttok

        def emit_expmask(gidx, it, ttok):
            h, qw, g, nk, ng = it
            kcs = kcs_of(it)
            sc_off = 1024 * (gidx % 2)
            wl2 = [ttok, pt_rot.get(gidx)] + (
                list(first_wait) if gidx == 0 else [])
            s_exp = S.do("scalar", lambda e, g2=gidx % 2, sc_off=sc_off, h=h,
                         nn=len(kcs) * 512, Pt=PtT: e.activation(
                             Pt[:, g2, 0:nn], psF[:, sc_off:sc_off + nn],
                             AF.Exp, scale=eighth[:], bias=biasAP(h)),
                         waits=wl2, inc="sp")
            sc_rot.set(gidx, s_exp)
            mops = [mask_list(qw, kc) for kc in kcs]
            mtok = [None] * len(kcs)

            def emit_mask(iis, fn):
                t = S.do("vector", fn, waits=[s_exp], inc="vp")
                for i3 in iis:
                    mtok[i3] = t

            i4 = 0
            while i4 < len(kcs):
                ml = mops[i4]
                if (len(kcs) == 2 and i4 == 0 and len(ml) == 1 and
                        len(mops[1]) == 1 and ml[0][0] == mops[1][0][0]):
                    a, b2 = ml[0], mops[1][0]
                    if (a[0] == "tt" and a[1] is b2[1] and
                            b2[2] == a[2] + 512):
                        emit_mask([0, 1], lambda e, g2=gidx % 2, mt=a[1],
                                  c0=a[2], Pt=PtT: e.tensor_tensor(
                                      Pt[:, g2, 0:1024], Pt[:, g2, 0:1024],
                                      mt[:, c0:c0 + 1024], ALU.mult))
                        i4 = 2
                        continue
                    if a[0] == "ts" and b2[0] == "ts" and a[1] is b2[1]:
                        emit_mask([0, 1], lambda e, g2=gidx % 2, sap=a[1],
                                  Pt=PtT: e.tensor_scalar(
                                      Pt[:, g2, 0:1024], Pt[:, g2, 0:1024],
                                      sap, None, ALU.mult))
                        i4 = 2
                        continue
                for mop in ml:
                    if mop[0] == "tt":
                        emit_mask([i4], lambda e, g2=gidx % 2, ii=i4,
                                  mt=mop[1], c0=mop[2], Pt=PtT:
                                  e.tensor_tensor(
                                      Pt[:, g2, ii * 512:(ii + 1) * 512],
                                      Pt[:, g2, ii * 512:(ii + 1) * 512],
                                      mt[:, c0:c0 + 512], ALU.mult))
                    else:
                        emit_mask([i4], lambda e, g2=gidx % 2, ii=i4,
                                  sap=mop[1], Pt=PtT: e.tensor_scalar(
                                      Pt[:, g2, ii * 512:(ii + 1) * 512],
                                      Pt[:, g2, ii * 512:(ii + 1) * 512],
                                      sap, None, ALU.mult))
                i4 += 1
            return s_exp, mtok

        def emit_ctx(gidx, it, s_exp, mtok):
            h, qw, g, nk, ng = it
            kcs = kcs_of(it)
            psC_off = 2048 + 512 * ((h * nqw + qw) % 2)
            pcw = pc_rot.get(h * nqw + qw) if g == 0 else None
            ttok2 = None
            for ii, kc in enumerate(kcs):
                wl3 = [s_exp, mtok[ii]] + ([pcw] if ii == 0 else [])
                for j in range(4):
                    last = (kc == nk - 1 and j == 3)
                    ttok2 = S.do("tensor", lambda e, g2=gidx % 2, ii=ii,
                                 kc=kc, j=j, psC_off=psC_off, h=h, nk=nk,
                                 Pt=PtT: e.matmul(
                                     psF[:, psC_off + j * 65:
                                         psC_off + (j + 1) * 65],
                                     Pt[:, g2, ii * 512 + j * 128:
                                        ii * 512 + (j + 1) * 128],
                                     rhsV(h, kc),
                                     start=(kc == 0 and ii == 0 and j == 0),
                                     stop=(kc == nk - 1 and j == 3)),
                                 waits=wl3 if j == 0 else (),
                                 inc="tp" if last else None)
            pt_rot.set(gidx, ttok2)
            if g == ng - 1:
                S.do("vector", lambda e, psC_off=psC_off, rcp4=rcpT:
                     e.reciprocal(rcp4[:, 0:4],
                                  AP(psF, psC_off + 64, [[3072, 128], [65, 4]])),
                     waits=[ttok2])
                pc_rot.set(h * nqw + qw + 2,
                           S.do("vector", lambda e, psC_off=psC_off, h=h, qw=qw,
                                rcp4=rcpT: e.tensor_tensor(
                                    evac_ap(h, qw),
                                    AP(psF, psC_off, [[3072, 128], [65, 4], [1, 64]]),
                                    AP(rcp4, 0, [[8, 128], [1, 4], [0, 64]]),
                                    ALU.mult), inc="vp"))

        st = emit_scores(0, items[0])
        for i, it in enumerate(items):
            em = emit_expmask(i, it, st)
            if i + 1 < len(items):
                st = emit_scores(i + 1, items[i + 1])
            emit_ctx(i, it, *em)

    xn_fm = sbuf("xn_fm", [128, DCH, T], BF16)
    xq0_fm = sbuf("xq0_fm", [128, DCH, TOWN], BF16)
    xldb = sbuf("xldb", [128, 4, D], F32)
    xn_tm = sbuf("xn_tm", [128, 2, D], F32)
    xnb_tm = sbuf("xnb_tm", [128, 2, D], BF16)
    q8 = sbuf("q8t", [128, 2, D], FP8)
    q8b = sbuf("q8b", [128, 2, D], BF16)

    v_zero = S.do("vector", lambda e: e.memset(xq0_fm[:], 0.0),
                  waits=[v_consts], inc="vp")
    xn_rot = Rot(2)
    xn2_rot = Rot(2)
    xnb_rot = Rot(2)
    q8_rot = Rot(2)
    q8b_rot = Rot(2)

    x_tok = [None] * NCHUNK
    for i in range(4):
        x_tok[i] = S.dma("sync", xldb[:, i, :], x_in[i * 128:(i + 1) * 128, :],
                         waits=[v_consts], q=f"qs{2 + i % 2}")

    def p1_stats(i):
        return ln_stats(xldb[:, i % 4, :], [x_tok[i]], i % 2)

    def p1_consume(i, v_st):
        b = i % 4
        src = xldb[:, b, :]
        s_xn = S.do("scalar", lambda e, src=src, sl=i % 2: e.activation(
            xn_tm[:, sl, :], src, AF.Identity, scale=statT[:, sl, 27:28],
            bias=statT[:, sl, 28:29]),
            waits=[v_st, xn_rot.get(i), xn2_rot.get(i)], inc="sp")
        ln_guard.set(i, s_xn)
        if i + 4 < NCHUNK:
            x_tok[i + 4] = S.dma("sync", xldb[:, b, :],
                                 x_in[(i + 4) * 128:(i + 5) * 128, :],
                                 waits=[s_xn], q=f"qs{2 + b % 2}")
        v_cast = S.do("vector", lambda e, b2=i % 2: e.tensor_copy(
            xnb_tm[:, b2, :], xn_tm[:, b2, :]),
            waits=[s_xn, xnb_rot.get(i)], inc="vp")
        xn_rot.set(i, v_cast)
        ev, tl = transpose_tile(
            lambda cc, b2=i % 2: xnb_tm[:, b2, cc * 128:(cc + 1) * 128],
            lambda hg, nb, i=i: AP(xn_fm, hg * 8 * T + i * 128,
                                   [[DCH * T, 128], [T, nb], [1, 128]]),
            DCH, v_cast)
        xnb_rot.set(i, tl)
        s_q8 = S.do("scalar", lambda e, b2=i % 2: e.activation(
            q8[:, b2, :], xn_tm[:, b2, :], AF.Identity, scale=sc_misc[:, 0:1]),
            waits=[s_xn, q8_rot.get(i)], inc="sp")
        xn2_rot.set(i, s_q8)
        selcol = 0 if i < 8 else 1
        v_q8 = S.do("vector", lambda e, b2=i % 2, sc=selcol: e.tensor_scalar(
            q8b[:, b2, :], q8[:, b2, :], halfc[:, sc:sc + 1], None, ALU.mult),
            waits=[s_q8, q8b_rot.get(i)], inc="vp")
        q8_rot.set(i, v_q8)
        loc = i % 8
        lasttp = None
        for hg in range(2):
            key = TRC[0]
            TRC[0] += 1
            slot = key % 2
            tptok = None
            for c in range(8):
                cc = hg * 8 + c
                tptok = S.do("tensor", lambda e, b2=i % 2, cc=cc, c=c, slot=slot:
                             e.transpose(
                                 psB16[:, slot * 1024 + c * 128:slot * 1024 + (c + 1) * 128],
                                 q8b[:, b2, cc * 128:(cc + 1) * 128], ident[:]),
                             waits=[v_q8, psrot.get(key)] if c == 0 else (),
                             inc="tp" if c == 7 else None)
            ev2 = S.do("vector", lambda e, loc=loc, hg=hg, slot=slot: e.tensor_tensor(
                AP(xq0_fm, hg * 8 * TOWN + loc * 128,
                   [[DCH * TOWN, 128], [TOWN, 8], [1, 128]]),
                AP(xq0_fm, hg * 8 * TOWN + loc * 128,
                   [[DCH * TOWN, 128], [TOWN, 8], [1, 128]]),
                AP(psB16, slot * 1024, [[2048, 128], [128, 8], [1, 128]]),
                ALU.add), waits=[tptok, v_zero], inc="vp")
            psrot.set(key, ev2)
            lasttp = tptok
        q8b_rot.set(i, lasttp)

    vst_p1 = p1_stats(0)
    for i in range(NCHUNK):
        nxt = p1_stats(i + 1) if i + 1 < NCHUNK else None
        p1_consume(i, vst_p1)
        vst_p1 = nxt
    v_p1 = S.now("vp")
    bar("bar1")
    tap("xn_fm", xn_fm)
    tap("xq0_fm", xq0_fm)
    sbuf_free("xldb", "xn_tm", "xnb_tm", "q8t", "q8b",
               "absacc", "bigbuf")
    b_p1 = S.bnd()

    Q_fm = sbuf("Q_fm", [128, 8, T], BF16)
    K_fm = sbuf("K_fm", [128, 2, T], BF16)
    V_st = sbuf("V_st", [128, NCHUNK, 260], BF16)
    wld = sbuf("wld", [128, 2, 512], F32)
    wbf = sbuf("wbf", [128, 2, DCH, 256], BF16)
    ptm = sbuf("ptm", [128, 2, 512], BF16)
    prp = sbuf("prp", [128, 2, 512], BF16)
    rsc = sbuf("rsc", [128, 2, 512], BF16)
    sqv = sbuf("sqv", [128, 512], BF16)
    qn2t = sbuf("qn2t", [128, 16], F32)

    wld_rot = Rot(2)
    wbf_rot = Rot(2)
    pj_rot = Rot(2)
    ptm_rot = Rot(2)
    prp_rot = Rot(2)

    def stream_w(wsrc, nk, col0, width, slot, extra=None, ldrot=None, bfrot=None,
                 ldbuf=None, bfbuf=None, rows128=True):
        ldrot = ldrot or wld_rot
        bfrot = bfrot or wbf_rot
        ldbuf = ldbuf if ldbuf is not None else wld
        bfbuf = bfbuf if bfbuf is not None else wbf
        wtoks = {}
        for kc in range(nk):
            nr = 128 if (rows128 or kc < nk - 1) else 1
            tok = S.dma("sync", ldbuf[0:nr, kc % 2, 0:width],
                        wsrc[kc * 128:kc * 128 + nr, col0:col0 + width],
                        waits=[ldrot.get(kc), extra if kc == 0 else None],
                        q=f"qs{4 + kc % 2}")
            wtoks[kc] = S.do("scalar", lambda e, kc=kc, nr=nr, w=width, slot=slot,
                             bfbuf=bfbuf, ldbuf=ldbuf: e.copy(
                                 bfbuf[0:nr, slot, kc, 0:w],
                                 ldbuf[0:nr, kc % 2, 0:w]),
                             waits=[tok, bfrot.get(slot) if kc == 0 else None],
                             inc="sp")
            ldrot.set(kc, wtoks[kc])
        return wtoks

    v_ones = None
    for tch in range(NCHUNK):
        v_ones = S.do("vector", lambda e, tch=tch: e.memset(
            AP(V_st, tch * 260 + 64, [[NCHUNK * 260, 128], [65, 4]]), 1.0), inc="vp")

    def rope_block(b, nh4, tch, v_in):
        x1p = AP(ptm, b * 512, [[1024, 128], [64, nh4], [1, 32]])
        x2p = AP(ptm, b * 512 + 32, [[1024, 128], [64, nh4], [1, 32]])
        o1p = AP(prp, b * 512, [[1024, 128], [64, nh4], [1, 32]])
        o2p = AP(prp, b * 512 + 32, [[1024, 128], [64, nh4], [1, 32]])
        cosb = AP(cosT, tch * 32, [[512, 128], [0, nh4], [1, 32]])
        sinb = AP(sinT, tch * 32, [[512, 128], [0, nh4], [1, 32]])
        sA = AP(rsc, b * 512, [[1024, 128], [32, nh4], [1, 32]])
        sB = AP(rsc, b * 512 + nh4 * 32, [[1024, 128], [32, nh4], [1, 32]])
        S.do("vector", lambda e, a=sA, x=x1p, c=cosb: e.tensor_tensor(a, x, c,
                                                                      ALU.mult),
             waits=[v_in, prp_rot.get(tch)])
        S.do("vector", lambda e, a=sB, x=x2p, c=sinb: e.tensor_tensor(a, x, c,
                                                                      ALU.mult))
        S.do("vector", lambda e, o=o1p, a=sA, bq=sB: e.tensor_tensor(o, a, bq,
                                                                     ALU.subtract))
        S.do("vector", lambda e, a=sA, x=x2p, c=cosb: e.tensor_tensor(a, x, c,
                                                                      ALU.mult))
        S.do("vector", lambda e, a=sB, x=x1p, c=sinb: e.tensor_tensor(a, x, c,
                                                                      ALU.mult))
        return S.do("vector", lambda e, o=o2p, a=sA, bq=sB: e.tensor_tensor(
            o, a, bq, ALU.add), inc="vp")

    pidx = 0
    for proj, wsrc, width, ews in (("q", l0wq, 256, 4), ("k", l0wk, 256, 1),
                                   ("v", l0wv, 256, 1)):
        for ew in range(ews):
            slot = pidx % 2
            v_w = stream_w(wsrc, DCH, ew * width, width, slot,
                           extra=b_p1[1] if pidx == 0 else None)
            last_mm_slot = None
            for tch in range(NCHUNK):
                idx = pidx * NCHUNK + tch
                ps_off = (idx % 2) * 512
                ps = psF[:, ps_off:ps_off + width]
                for kc in range(DCH):
                    last_mm_slot = S.do("tensor", lambda e, kc=kc, tch=tch, slot=slot,
                                        ps=ps, width=width: e.matmul(
                                            ps, xn_fm[:, kc, tch * 128:(tch + 1) * 128],
                                            wbf[:, slot, kc, 0:width],
                                            start=(kc == 0), stop=(kc == DCH - 1)),
                                        waits=[v_w[kc]] +
                                              ([pj_rot.get(idx), v_p1]
                                               if kc == 0 else []),
                                        inc="tp" if kc == DCH - 1 else None)
                b = tch % 2
                if proj in ("q", "k"):
                    nh4 = width // 64
                    v_e = S.do("vector", lambda e, b=b, ps=ps, width=width:
                               e.tensor_copy(ptm[:, b, 0:width], ps),
                               waits=[last_mm_slot, ptm_rot.get(tch)], inc="vp")
                    pj_rot.set(idx, v_e)
                    v_r = rope_block(b, nh4, tch, v_e)
                    ptm_rot.set(tch, v_r)
                    S.do("vector", lambda e, b=b, width=width: e.tensor_tensor(
                        sqv[:, 0:width], prp[:, b, 0:width], prp[:, b, 0:width],
                        ALU.mult), waits=[v_r])
                    if proj == "q":
                        red_ap = AP(sqv, 0, [[512, 128], [64, 4], [1, 64]])
                        nred, a0 = 4, ew * 4
                        accd = qn2acc
                    else:
                        red_ap = AP(sqv, 0, [[512, 128], [64, 4], [0, 4], [1, 64]])
                        nred, a0 = 16, 0
                        accd = kn2acc
                    S.do("vector", lambda e, red_ap=red_ap, nred=nred: e.reduce_sum(
                        qn2t[:, 0:nred], red_ap, axis=AXX))
                    if tch == 0:
                        S.do("vector", lambda e, accd=accd, a0=a0, nred=nred:
                             e.tensor_copy(accd[:, a0:a0 + nred], qn2t[:, 0:nred]),
                             inc="vp")
                    else:
                        S.do("vector", lambda e, accd=accd, a0=a0, nred=nred:
                             e.tensor_tensor(accd[:, a0:a0 + nred],
                                             accd[:, a0:a0 + nred],
                                             qn2t[:, 0:nred], ALU.max), inc="vp")
                    nblk = width // 128
                    dst_fm = Q_fm if proj == "q" else K_fm
                    dch = 8 if proj == "q" else 2
                    c0 = ew * 2 if proj == "q" else 0
                    ev, tl = transpose_tile(
                        lambda cc, b=b: prp[:, b, cc * 128:(cc + 1) * 128],
                        lambda hg, nb, tch=tch, dst_fm=dst_fm, dch=dch, c0=c0:
                        AP(dst_fm, (c0 + hg * 8) * T + tch * 128,
                           [[dch * T, 128], [T, nb], [1, 128]]),
                        nblk, S.now("vp"))
                    prp_rot.set(tch, tl)
                else:
                    v_e = S.do("vector", lambda e, tch=tch, ps_off=ps_off:
                               e.tensor_copy(
                                   AP(V_st, tch * 260,
                                      [[NCHUNK * 260, 128], [65, 4], [1, 64]]),
                                   AP(psF, ps_off, [[3072, 128], [64, 4], [1, 64]])),
                               waits=[last_mm_slot, v_ones], inc="vp")
                    pj_rot.set(idx, v_e)
            wbf_rot.set(slot, last_mm_slot)
            pidx += 1
    v_p2 = S.now("vp")
    kn2_perm = AP(kn2acc, 0, [[32, 128], [8, 2], [0, 4], [4, 2]])
    v_ct0 = cterm_pipe(qn2acc, 0, kn2_perm, cterm0, 0, 16, [v_p2])
    bar("bar2")
    tap("Q_fm", Q_fm)
    tap("K_fm", K_fm)
    tap("V_st", V_st)
    tap("cterm0", cterm0)

    sbuf_free("xn_fm", "wld", "wbf", "ptm", "prp", "rsc", "sqv", "qn2t")
    b_p2 = S.bnd()
    g0 = sbuf("g0", [128, OCHUNK, D], F32)
    w8ld = sbuf("w8ld", [128, 2, 512], FP8)
    w8bf = sbuf("w8bf", [128, 2, DCH, 512], BF16)
    w8bf_rot = Rot(2)
    w8rot = Rot(2)
    ps3_rot = Rot(2)

    s_g0 = None
    for ow in range(4):
        slot = ow % 2
        w8c = {}
        for kc in range(DCH):
            tok = S.dma("sync", w8ld[:, kc % 2, :],
                        l0w8[kc * 128:(kc + 1) * 128, ow * 512:(ow + 1) * 512],
                        waits=[w8rot.get(kc),
                               b_p2[1] if ow == 0 and kc == 0 else None],
                        q=f"qs{kc % 2}")
            w8c[kc] = S.do("vector", lambda e, kc=kc, slot=slot: e.tensor_copy(
                w8bf[:, slot, kc, :], w8ld[:, kc % 2, :]),
                waits=[tok, w8bf_rot.get(slot) if kc == 0 else None] +
                      (b_p2 if ow == 0 and kc == 0 else []), inc="vp")
            w8rot.set(kc, w8c[kc])
        last_mm = None
        for tch in range(OCHUNK):
            idx = ow * OCHUNK + tch
            ps_off = (idx % 2) * 512
            ps = psF[:, ps_off:ps_off + 512]
            for kc in range(DCH):
                last_mm = S.do("tensor", lambda e, kc=kc, tch=tch, slot=slot, ps=ps:
                               e.matmul(ps, xq0_fm[:, kc, tch * 128:(tch + 1) * 128],
                                        w8bf[:, slot, kc, :],
                                        start=(kc == 0), stop=(kc == DCH - 1)),
                               waits=[w8c[kc]] +
                                     ([ps3_rot.get(idx), v_p1] if kc == 0 else []),
                               inc="tp" if kc == DCH - 1 else None)
            s_g0 = S.do("scalar", lambda e, tch=tch, ow=ow, ps=ps: e.activation(
                g0[:, tch, ow * 512:(ow + 1) * 512], ps, AF.Silu,
                scale=sc_misc[:, 1:2]), waits=[last_mm], inc="sp")
            ps3_rot.set(idx, s_g0)
        w8bf_rot.set(slot, last_mm)
    bar("bar3")
    tap("g0", g0)
    tap("w8bf", w8bf)
    if "stop3" in taps:
        zed = sbuf("zed", [128, 512], F32)
        vz = S.do("vector", lambda e: e.memset(zed[:], 0.0))
        for tch in range(OCHUNK):
            for ow in range(4):
                S.dma("gpsimd", out[tch * 128:(tch + 1) * 128,
                                    ow * 512:(ow + 1) * 512], zed[:],
                      waits=[vz, S.now("sp")], q="qg")
        gfin = S.cnt["qg"]
        S.do("gpsimd", lambda e: e.nop(), waits=[("qg", gfin)])
        with nc.Block() as block:
            S.finalize(block)
        return nc, S

    sbuf_free("w8ld", "w8bf", "xq0_fm")
    b_p3 = S.bnd()
    ctx_q = sbuf("ctx_q", [128, NCHUNK, 1024], BF16)
    Pt = sbuf("Pt", [128, 2, 1024], BF16)
    rcp4 = sbuf("rcp4", [128, 8], F32)

    PERMQ = [0, 4, 1, 5, 2, 6, 3, 7, 8, 12, 9, 13, 10, 14, 11, 15]
    kv_of = lambda s: 2 * (s // 8) + (s % 2)
    attention(
        nheads=16, nqw=4, nkc=lambda qw: 4 * qw + 4,
        lhsK=lambda s, kc: K_fm[(s % 2) * 64:(s % 2) * 64 + 64,
                                s // 8, kc * 128:(kc + 1) * 128],
        rhsQ=lambda s, qw: Q_fm[(s % 2) * 64:(s % 2) * 64 + 64, s // 2,
                                qw * 512:(qw + 1) * 512],
        rhsV=lambda s, kc: V_st[:, kc, kv_of(s) * 65:kv_of(s) * 65 + 65],
        biasAP=lambda s: cterm0[:, s:s + 1],
        mask_list=lambda qw, kc: ([("tt", wmk, (kc - 4 * qw) * 512)]
                                  if 0 <= kc - 4 * qw < 4 else []),
        evac_ap=lambda s, qw: AP(ctx_q, 4 * qw * 1024 + PERMQ[s] * 64,
                                 [[NCHUNK * 1024, 128], [1024, 4], [1, 64]]),
        first_wait=[v_ct0, s_g0], PtT=Pt, rcpT=rcp4)
    bar("bar4")
    tap("ctx_q", ctx_q)

    sbuf_free("Q_fm", "K_fm", "V_st", "Pt", "rcp4")
    b_p4 = S.bnd()
    x1 = sbuf("x1", [128, OCHUNK, D], F32)
    wld2 = sbuf("wld2", [128, 2, 256], F32)
    wbf2 = sbuf("wbf2", [128, 2, DCH, 256], BF16)
    x0ld = sbuf("x0ld", [128, 2, 256], F32)
    wld2_rot = Rot(2)
    wbf2_rot = Rot(2)
    x0_rot = Rot(2)
    ps5_rot = Rot(2)

    for ow in range(8):
        slot = ow % 2
        w2c = {}
        for kc in range(DCH):
            tok = S.dma("sync", wld2[:, kc % 2, :],
                        l0wo[kc * 128:(kc + 1) * 128, ow * 256:(ow + 1) * 256],
                        waits=[wld2_rot.get(kc),
                               b_p4[1] if ow == 0 and kc == 0 else None],
                        q=f"qs{2 + kc % 2}")
            w2c[kc] = S.do("scalar", lambda e, kc=kc, slot=slot: e.copy(
                wbf2[:, slot, kc, :], wld2[:, kc % 2, :]),
                waits=[tok, wbf2_rot.get(slot) if kc == 0 else None] +
                      (b_p4 if ow == 0 and kc == 0 else []), inc="sp")
            wld2_rot.set(kc, w2c[kc])
        last_mm = None
        for pch in range(OCHUNK):
            idx = ow * OCHUNK + pch
            x0tok = S.dma("gpsimd", x0ld[:, pch % 2, :],
                          x_own[pch * 128:(pch + 1) * 128, ow * 256:(ow + 1) * 256],
                          waits=[x0_rot.get(idx),
                                 b_p4[0] if idx == 0 else None],
                          q=f"qg{1 + pch % 2}")
            ps_off = (idx % 2) * 512
            ps = psF[:, ps_off:ps_off + 256]
            for kq in range(DCH):
                last_mm = S.do("tensor", lambda e, kq=kq, pch=pch, slot=slot, ps=ps:
                               e.matmul(ps, ctx_q[:, kq, pch * 128:(pch + 1) * 128],
                                        wbf2[:, slot, kq, :],
                                        start=(kq == 0), stop=(kq == DCH - 1)),
                               waits=[w2c[kq]] +
                                     ([ps5_rot.get(idx)] if kq == 0 else []),
                               inc="tp" if kq == DCH - 1 else None)
            b = pch % 2
            S.do("vector", lambda e, b=b, pch=pch, ow=ow, ps=ps: e.tensor_tensor(
                x1[:, pch, ow * 256:(ow + 1) * 256], ps,
                g0[:, pch, ow * 256:(ow + 1) * 256], ALU.mult),
                waits=[last_mm, s_g0, x0tok])
            v_x1t = S.do("vector", lambda e, b=b, pch=pch, ow=ow: e.tensor_tensor(
                x1[:, pch, ow * 256:(ow + 1) * 256],
                x1[:, pch, ow * 256:(ow + 1) * 256], x0ld[:, b, :],
                ALU.add), inc="vp")
            ps5_rot.set(idx, v_x1t)
            x0_rot.set(idx, v_x1t)
        wbf2_rot.set(slot, last_mm)
    v_x1 = S.now("vp")
    bar("bar5")
    tap("x1", x1)

    sbuf_free("ctx_q", "wld2", "wbf2", "x0ld", "g0")
    b_p5 = S.bnd()
    xn1_fm = sbuf("xn1_fm", [128, 17, TOWN], BF16)
    xq1_fm = sbuf("xq1_fm", [128, DCH, TOWN], BF16)
    xn_tm6 = sbuf("xn_tm6", [128, 4, D], F32)
    xnb6 = sbuf("xnb6", [128, 2, D], BF16)
    xi8 = sbuf("xi8", [128, 2, D], I8)
    xq1b = sbuf("xq1b", [128, 2, D], BF16)

    v_onesrow = S.do("vector", lambda e: e.memset(xn1_fm[:, 16, :], 1.0),
                     waits=b_p5, inc="vp")
    xn6_rot = Rot(4)
    xn62_rot = Rot(4)
    xnb6_rot = Rot(2)
    xi8_rot = Rot(2)
    xq1b_rot = Rot(2)
    spill_toks = {}

    def p6_stats(tch):
        return ln_stats(x1[:, tch, :], [v_x1] + (b_p5 if tch == 0 else []),
                        tch % 2)

    def p6_consume(tch, v_st):
        b = tch % 4
        sl = tch % 2
        src = x1[:, tch, :]
        s_xn = S.do("scalar", lambda e, b=b, src=src, sl=sl: e.activation(
            xn_tm6[:, b, :], src, AF.Identity, scale=statT[:, sl, 27:28],
            bias=statT[:, sl, 28:29]),
            waits=[v_st, xn6_rot.get(tch), xn62_rot.get(tch)], inc="sp")
        spill_toks[tch] = S.dma("gpsimd", x1_scr[tch * 128:(tch + 1) * 128, :],
                                src, waits=[v_st], q="qg3")
        v_cast = S.do("vector", lambda e, b=b, b2=sl: e.tensor_copy(
            xnb6[:, b2, :], xn_tm6[:, b, :]), waits=[s_xn, xnb6_rot.get(tch)],
            inc="vp")
        xn6_rot.set(tch, v_cast)
        ev, tl = transpose_tile(
            lambda cc, b2=sl: xnb6[:, b2, cc * 128:(cc + 1) * 128],
            lambda hg, nb, tch=tch: AP(xn1_fm, hg * 8 * TOWN + tch * 128,
                                       [[17 * TOWN, 128], [TOWN, nb], [1, 128]]),
            DCH, v_cast)
        xnb6_rot.set(tch, tl)
        S.do("vector", lambda e, b=b, sl=sl: e.reduce_max(
            statT[:, sl, 29:30], xn_tm6[:, b, :], axis=AXX,
            apply_absolute_value=True), waits=[s_xn])
        S.do("vector", lambda e, sl=sl: e.tensor_scalar(
            statT[:, sl, 30:31], statT[:, sl, 29:30], 1e-5, None, ALU.max))
        S.do("vector", lambda e, sl=sl: e.reciprocal(
            statT[:, sl, 31:32], statT[:, sl, 30:31]))
        S.do("vector", lambda e, sl=sl: e.tensor_scalar(
            statT[:, sl, 32:33], statT[:, sl, 31:32], 127.0, None, ALU.mult))
        v_sc = S.do("vector", lambda e, tch=tch, sl=sl: e.tensor_scalar(
            rall[:, tch:tch + 1], statT[:, sl, 30:31], 1.0 / 127.0, None,
            ALU.mult), inc="vp")
        s_i8 = S.do("scalar", lambda e, b=b, b2=sl, sl=sl: e.activation(
            xi8[:, b2, :], xn_tm6[:, b, :], AF.Identity,
            scale=statT[:, sl, 32:33]),
            waits=[v_sc, xi8_rot.get(tch)], inc="sp")
        xn62_rot.set(tch, s_i8)
        ln_guard.set(tch, s_i8)
        v_bf = S.do("vector", lambda e, b2=sl: e.tensor_copy(
            xq1b[:, b2, :], xi8[:, b2, :]),
            waits=[s_i8, xq1b_rot.get(tch)], inc="vp")
        xi8_rot.set(tch, v_bf)
        ev, tl = transpose_tile(
            lambda cc, b2=sl: xq1b[:, b2, cc * 128:(cc + 1) * 128],
            lambda hg, nb, tch=tch: AP(xq1_fm, hg * 8 * TOWN + tch * 128,
                                       [[DCH * TOWN, 128], [TOWN, nb], [1, 128]]),
            DCH, v_bf)
        xq1b_rot.set(tch, tl)

    vst_p6 = p6_stats(0)
    for tch in range(OCHUNK):
        nxt = p6_stats(tch + 1) if tch + 1 < OCHUNK else None
        p6_consume(tch, vst_p6)
        vst_p6 = nxt
    tap("rall", rall)
    S.do("vector", lambda e: e.tensor_tensor(
        sc1a[:], rall[:], AP(mu128, 4, [[8, 128], [0, 8]]), ALU.mult),
        waits=[v_mu128, S.now("vp")])
    v_p6 = S.do("vector", lambda e: e.tensor_tensor(
        sc3a[:], rall[:], AP(mu128, 5, [[8, 128], [0, 8]]), ALU.mult), inc="vp")
    bar("bar6")
    tap("xn1_fm", xn1_fm)
    tap("xq1_fm", xq1_fm)

    sbuf_free("xn_tm6", "xnb6", "xi8", "xq1b", "x1")
    b_p6 = S.bnd()
    spill_all = spill_toks[OCHUNK - 1]
    Q_fm1 = sbuf("Q_fm1", [128, 16, TOWN], BF16)
    wld3 = sbuf("wld3", [128, 2, 512], F32)
    wbf3 = sbuf("wbf3", [128, 2, 17, 512], BF16)
    ptm3 = sbuf("ptm3", [128, 2, 512], BF16)
    sqv3 = sbuf("sqv3", [128, 512], BF16)
    qn2t3 = sbuf("qn2t3", [128, 8], F32)
    wld3_rot = Rot(2)
    wbf3_rot = Rot(2)
    pj3_rot = Rot(2)
    ptm3_rot = Rot(2)
    ag_fin = [None, None]

    pidx3 = 0
    for proj, wsrc in (("q", wqtI), ("k", wktI), ("v", wvtI)):
        for ew in range(4):
            slot = pidx3 % 2
            w3c = {}
            for kc in range(17):
                nr = 128 if kc < 16 else 1
                tok = S.dma("sync", wld3[0:nr, kc % 2, :],
                            wsrc[kc * 128:kc * 128 + nr, ew * 512:(ew + 1) * 512],
                            waits=[wld3_rot.get(kc), spill_all,
                                   b_p6[1] if pidx3 == 0 and kc == 0 else None],
                            q=f"qs{4 + kc % 2}")
                w3c[kc] = S.do("scalar", lambda e, kc=kc, nr=nr, slot=slot:
                               e.copy(wbf3[0:nr, slot, kc, :],
                                      wld3[0:nr, kc % 2, :]),
                               waits=[tok, wbf3_rot.get(slot) if kc == 0 else None] +
                                     (b_p6 + [spill_all] if pidx3 == 0 and kc == 0
                                      else []), inc="sp")
                wld3_rot.set(kc, w3c[kc])
            last_mm = None
            for tch in range(OCHUNK):
                idx = pidx3 * OCHUNK + tch
                ps_off = (idx % 2) * 512
                ps = psF[:, ps_off:ps_off + 512]
                for kc in range(17):
                    lhs = (xn1_fm[:, kc, tch * 128:(tch + 1) * 128] if kc < 16
                           else xn1_fm[0:1, 16, tch * 128:(tch + 1) * 128])
                    rhs = (wbf3[:, slot, kc, :] if kc < 16
                           else wbf3[0:1, slot, 16, :])
                    last_mm = S.do("tensor", lambda e, lhs=lhs, rhs=rhs, kc=kc,
                                   ps=ps: e.matmul(ps, lhs, rhs, start=(kc == 0),
                                                   stop=(kc == 16)),
                                   waits=[w3c[kc]] +
                                         ([pj3_rot.get(idx), v_onesrow,
                                           v_p6] if kc == 0 else []),
                                   inc="tp" if kc == 16 else None)
                b = tch % 2
                v_e = S.do("vector", lambda e, b=b, ps=ps: e.tensor_copy(
                    ptm3[:, b, :], ps), waits=[last_mm, ptm3_rot.get(tch)],
                    inc="vp")
                pj3_rot.set(idx, v_e)
                if proj == "q":
                    S.do("vector", lambda e, b=b: e.tensor_tensor(
                        sqv3[:], ptm3[:, b, :], ptm3[:, b, :], ALU.mult),
                        waits=[v_e])
                    S.do("vector", lambda e: e.reduce_sum(
                        qn2t3[:, 0:8], AP(sqv3, 0, [[512, 128], [64, 8], [1, 64]]),
                        axis=AXX))
                    if tch == 0:
                        S.do("vector", lambda e, ew=ew: e.tensor_copy(
                            qn2acc[:, ew * 8:ew * 8 + 8], qn2t3[:, 0:8]), inc="vp")
                    else:
                        S.do("vector", lambda e, ew=ew: e.tensor_tensor(
                            qn2acc[:, ew * 8:ew * 8 + 8],
                            qn2acc[:, ew * 8:ew * 8 + 8], qn2t3[:, 0:8], ALU.max),
                            inc="vp")
                    ev, tl = transpose_tile(
                        lambda cc, b=b: ptm3[:, b, cc * 128:(cc + 1) * 128],
                        lambda hg, nb, tch=tch, ew=ew: AP(
                            Q_fm1, ew * 4 * TOWN + tch * 128,
                            [[16 * TOWN, 128], [TOWN, nb], [1, 128]]),
                        4, S.now("vp"))
                    ptm3_rot.set(tch, tl)
                else:
                    col0 = (0 if proj == "k" else D) + ew * 512
                    gt = S.dma("gpsimd", ag_in[tch * 128:(tch + 1) * 128,
                                               col0:col0 + 512],
                               ptm3[:, b, :], waits=[v_e], q=f"qg{4 + b}")
                    ag_fin[b] = gt
                    ptm3_rot.set(tch, gt)
            wbf3_rot.set(slot, last_mm)
            pidx3 += 1
    cc_ag = S.do("gpsimd", lambda e: e.collective_compute(
        "AllGather", ALU.bypass, replica_groups=[[0, 1], [2, 3], [4, 5], [6, 7]],
        ins=[ag_in.ap().opt()], outs=[ag_out.ap().opt()]),
        waits=ag_fin, inc="cc")
    bar("bar7")
    tap("Q_fm1", Q_fm1)

    sbuf_free("wld3", "wbf3", "ptm3", "sqv3", "qn2t3", "xn1_fm")
    b_p7 = S.bnd()
    w13ld = sbuf("w13ld", [128, 4, 512], F32)
    t8b = sbuf("t8b", [128, 4, 512], I8)
    w13bf = sbuf("w13bf", [128, 2, 2, DCH, 512], BF16)
    hbuf = sbuf("hbuf", [128, 2, 512], F32)
    t1b = sbuf("t1b", [128, 2, 512], F32)
    w13ld_rot = Rot(4)
    t8_rot = Rot(4)
    w13_rot = Rot(2)
    ps8_rot = Rot(4)
    hb_rot = Rot(2)
    t1_rot = Rot(2)
    h_fin = [None, None]

    def tern8(hw, ci0, ci1):
        slot = hw % 2
        vtok = None
        for ci in range(ci0, ci1):
            which, kc = divmod(ci, DCH)
            wsrc = w1tI if which == 0 else w3tI
            qi = hw * 32 + ci
            tok = S.dma("sync", w13ld[:, qi % 4, :],
                        wsrc[kc * 128:(kc + 1) * 128, hw * 512:(hw + 1) * 512],
                        waits=[w13ld_rot.get(qi),
                               b_p7[1] if qi == 0 else None],
                        q=f"qs{qi % 4}")
            s_t = S.do("scalar", lambda e, b4=qi % 4, mucol=which: e.activation(
                t8b[:, b4, :], w13ld[:, b4, :], AF.Identity,
                scale=mu128[:, mucol:mucol + 1]),
                waits=[tok, t8_rot.get(qi)] +
                      (b_p7 if qi == 0 else []), inc="sp")
            w13ld_rot.set(qi, s_t)
            vtok = S.do("vector", lambda e, b4=qi % 4, slot=slot, which=which,
                        kc=kc: e.tensor_scalar(
                            w13bf[:, slot, which, kc, :], t8b[:, b4, :],
                            -1.0, 1.0, ALU.max, ALU.min),
                        waits=[s_t, w13_rot.get(slot) if qi % 32 == 0 else None],
                        inc="vp")
            t8_rot.set(qi, vtok)
            vt8[(hw, which, kc)] = vtok
        return vtok

    vt8 = {}
    tern8(0, 0, 32)
    for hw in range(11):
        slot = hw % 2
        last1 = last3 = None
        for tch in range(OCHUNK):
            if hw + 1 < 11:
                tern8(hw + 1, 4 * tch, 4 * tch + 4)
            idx = hw * OCHUNK + tch
            o1 = (idx % 2) * 1024
            ps1 = psF[:, o1:o1 + 512]
            ps3 = psF[:, o1 + 512:o1 + 1024]
            for kc in range(DCH):
                last1 = S.do("tensor", lambda e, kc=kc, tch=tch, slot=slot, ps1=ps1:
                             e.matmul(ps1, xq1_fm[:, kc, tch * 128:(tch + 1) * 128],
                                      w13bf[:, slot, 0, kc, :],
                                      start=(kc == 0), stop=(kc == DCH - 1)),
                             waits=[vt8[(hw, 0, kc)]] +
                                   ([ps8_rot.get(2 * idx)] if kc == 0 else []),
                             inc="tp" if kc == DCH - 1 else None)
            for kc in range(DCH):
                last3 = S.do("tensor", lambda e, kc=kc, tch=tch, slot=slot, ps3=ps3:
                             e.matmul(ps3, xq1_fm[:, kc, tch * 128:(tch + 1) * 128],
                                      w13bf[:, slot, 1, kc, :],
                                      start=(kc == 0), stop=(kc == DCH - 1)),
                             waits=[vt8[(hw, 1, kc)]] +
                                   ([ps8_rot.get(2 * idx + 1)] if kc == 0 else []),
                             inc="tp" if kc == DCH - 1 else None)
            b = tch % 2
            s_si = S.do("scalar", lambda e, b=b, tch=tch, ps1=ps1: e.activation(
                t1b[:, b, :], ps1, AF.Silu, scale=sc1a[:, tch:tch + 1]),
                waits=[last1, t1_rot.get(tch)], inc="sp")
            ps8_rot.set(2 * idx, s_si)
            v_t3 = S.do("vector", lambda e, b=b, tch=tch, ps3=ps3: e.tensor_scalar(
                hbuf[:, b, :], ps3, sc3a[:, tch:tch + 1], None, ALU.mult),
                waits=[last3, s_si, hb_rot.get(tch)], inc="vp")
            ps8_rot.set(2 * idx + 1, v_t3)
            S.do("vector", lambda e, b=b: e.tensor_tensor(
                hbuf[:, b, :], t1b[:, b, :], hbuf[:, b, :], ALU.mult))
            S.do("vector", lambda e, b=b: e.reduce_max(
                statT[:, 0, 34:35], hbuf[:, b, :], axis=AXX,
                apply_absolute_value=True))
            if hw == 0:
                v_am = S.do("vector", lambda e, tch=tch: e.tensor_copy(
                    am2[:, tch:tch + 1], statT[:, 0, 34:35]), inc="vp")
            else:
                v_am = S.do("vector", lambda e, tch=tch: e.tensor_tensor(
                    am2[:, tch:tch + 1], am2[:, tch:tch + 1], statT[:, 0, 34:35],
                    ALU.max), inc="vp")
            t1_rot.set(tch, v_am)
            ht = S.dma("gpsimd", h_scr[tch * 128:(tch + 1) * 128,
                                       hw * 512:(hw + 1) * 512],
                       hbuf[:, b, :], waits=[v_am], q=f"qg{6 + b}")
            h_fin[b] = ht
            hb_rot.set(tch, ht)
        w13_rot.set(slot, last3)
    S.do("vector", lambda e: e.tensor_scalar(scq2[:], am2[:], 1e-5, None, ALU.max),
         waits=[S.now("vp")])
    S.do("vector", lambda e: e.tensor_scalar(scg2[:], scq2[:], 1.0 / 127.0, None,
                                             ALU.mult))
    S.do("vector", lambda e: e.tensor_tensor(
        scg2[:], scg2[:], AP(mu128, 6, [[8, 128], [0, 8]]), ALU.mult),
        waits=[v_mu128])
    S.do("vector", lambda e: e.reciprocal(scq2[:], scq2[:]))
    v_scq = S.do("vector", lambda e: e.tensor_scalar(scq2[:], scq2[:], 127.0, None,
                                                     ALU.mult), inc="vp")

    bar("bar8")
    sbuf_free("w13ld", "t8b", "w13bf", "hbuf", "t1b", "xq1_fm")
    b_p8 = S.bnd()
    K_fm1 = sbuf("K_fm1", [128, 8, T], BF16)
    V_st1 = sbuf("V_st1", [128, NCHUNK, 1040], BF16)
    ctx1 = sbuf("ctx1", [128, OCHUNK, D], BF16)
    Pt9 = sbuf("Pt9", [128, 2, 1024], BF16)
    rcp49 = sbuf("rcp49", [128, 8], F32)
    kvld = sbuf("kvld", [128, 2, 1024], BF16)
    vvld = sbuf("vvld", [128, 2, 1024], BF16)
    sqv9 = sbuf("sqv9", [128, 1024], BF16)
    kn2t9 = sbuf("kn2t9", [128, 16], F32)
    kv_rot = Rot(2)
    vv_rot = Rot(2)

    for grp in range(2):
        for kc in range(NCHUNK):
            ktok = S.dma("sync", kvld[:, kc % 2, :],
                         ag_out[kc * 128:(kc + 1) * 128,
                                grp * 1024:grp * 1024 + 1024],
                         waits=[kv_rot.get(kc), cc_ag] + h_fin +
                               [b_p8[1] if grp == 0 and kc == 0 else None],
                         q=f"qs{4 + kc % 2}")
            vtok2 = S.dma("sync", vvld[:, kc % 2, :],
                          ag_out[kc * 128:(kc + 1) * 128,
                                 D + grp * 1024:D + grp * 1024 + 1024],
                          waits=[vv_rot.get(kc), cc_ag] + h_fin,
                          q=f"qs{6 + kc % 2}")
            S.do("vector", lambda e, b=kc % 2: e.tensor_tensor(
                sqv9[:], kvld[:, b, :], kvld[:, b, :], ALU.mult),
                waits=[ktok] + (b_p8 if grp == 0 and kc == 0 else []))
            S.do("vector", lambda e: e.reduce_sum(
                kn2t9[:, 0:16], AP(sqv9, 0, [[1024, 128], [64, 16], [1, 64]]),
                axis=AXX))
            if kc == 0:
                S.do("vector", lambda e, grp=grp: e.tensor_copy(
                    kn2acc[:, grp * 16:grp * 16 + 16], kn2t9[:, 0:16]), inc="vp")
            else:
                S.do("vector", lambda e, grp=grp: e.tensor_tensor(
                    kn2acc[:, grp * 16:grp * 16 + 16],
                    kn2acc[:, grp * 16:grp * 16 + 16], kn2t9[:, 0:16], ALU.max),
                    inc="vp")
            ev, tl = transpose_tile(
                lambda cc, b=kc % 2: kvld[:, b, cc * 128:(cc + 1) * 128],
                lambda hg, nb, kc=kc: AP(K_fm1, hg * 8 * T + kc * 128,
                                         [[8 * T, 128], [T, nb], [1, 128]]),
                8, S.now("vp"))
            kv_rot.set(kc, tl)
            v_v1 = S.do("vector", lambda e, b=kc % 2, kc=kc: e.tensor_scalar(
                AP(V_st1, kc * 1040, [[NCHUNK * 1040, 128], [65, 16], [1, 64]]),
                vvld[:, b, :], alive[:, kc:kc + 1], None, ALU.mult),
                waits=[vtok2, S.now("tp")], inc="vp")
            v_v2 = S.do("vector", lambda e, kc=kc: e.tensor_copy(
                AP(V_st1, kc * 1040 + 64, [[NCHUNK * 1040, 128], [65, 16]]),
                AP(alive, kc, [[16, 128], [0, 16]])), inc="vp")
            vv_rot.set(kc, v_v2)
        v_ct1 = cterm_pipe(qn2acc, grp * 16,
                           AP(kn2acc, grp * 16, [[32, 128], [1, 16]]),
                           cterm1, grp * 16, 16, [S.now("vp")])
        attention(
            nheads=16, nqw=2, nkc=lambda qw: 12 if qw == 0 else NCHUNK,
            lhsK=lambda lh, kc: K_fm1[(lh % 2) * 64:(lh % 2) * 64 + 64, lh // 2,
                                      kc * 128:(kc + 1) * 128],
            rhsQ=lambda lh, qw, grp=grp: Q_fm1[
                ((grp * 16 + lh) % 2) * 64:((grp * 16 + lh) % 2) * 64 + 64,
                (grp * 16 + lh) // 2, qw * 512:(qw + 1) * 512],
            rhsV=lambda lh, kc: V_st1[:, kc, lh * 65:lh * 65 + 65],
            biasAP=lambda lh, grp=grp: cterm1[:, grp * 16 + lh:grp * 16 + lh + 1],
            mask_list=lambda qw, kc: (
                ([("tt", w1mk, (kc - 4 * qw) * 512)]
                 if 0 <= kc - 4 * qw < 4 else []) +
                ([("tt", wmk, (kc - 8 - 4 * qw) * 512)]
                 if 0 <= kc - 8 - 4 * qw < 4 else []) +
                ([("ts", zA[:, 0:1])] if 4 * qw + 4 <= kc < 8 else [])),
            evac_ap=lambda lh, qw, grp=grp: AP(
                ctx1, 4 * qw * D + (grp * 16 + lh) * 64,
                [[OCHUNK * D, 128], [D, 4], [1, 64]]),
            first_wait=[v_ct1], PtT=Pt9, rcpT=rcp49)
    bar("bar9")
    tap("ctx1", ctx1)

    sbuf_free("K_fm1", "V_st1", "Pt9", "rcp49", "kvld", "vvld", "sqv9",
               "kn2t9", "Q_fm1")
    b_p9 = S.bnd()
    ctx1f = sbuf("ctx1f", [128, 17, TOWN], BF16)
    s1 = sbuf("s1", [128, OCHUNK, D], F32)
    wld4 = sbuf("wld4", [128, 2, 512], F32)
    wbf4 = sbuf("wbf4", [128, 2, 17, 512], BF16)
    wld4_rot = Rot(2)
    wbf4_rot = Rot(2)
    ps10_rot = Rot(2)
    v_or = S.do("vector", lambda e: e.memset(ctx1f[0:1, 16, :], 1.0),
                waits=b_p9, inc="vp")
    v_ctx9 = S.now("vp")
    for tch in range(OCHUNK):
        transpose_tile(
            lambda cc, tch=tch: ctx1[:, tch, cc * 128:(cc + 1) * 128],
            lambda hg, nb, tch=tch: AP(ctx1f, hg * 8 * TOWN + tch * 128,
                                       [[17 * TOWN, 128], [TOWN, nb], [1, 128]]),
            DCH, v_ctx9)
    v_ctxf = S.now("vp")

    v_s1d = None
    for ow in range(4):
        slot = ow % 2
        w4c = {}
        for kc in range(17):
            nr = 128 if kc < 16 else 1
            tok = S.dma("sync", wld4[0:nr, kc % 2, :],
                        wotI[kc * 128:kc * 128 + nr, ow * 512:(ow + 1) * 512],
                        waits=[wld4_rot.get(kc),
                               b_p9[1] if ow == 0 and kc == 0 else None],
                        q=f"qs{kc % 2}")
            w4c[kc] = S.do("scalar", lambda e, kc=kc, nr=nr, slot=slot: e.copy(
                wbf4[0:nr, slot, kc, :], wld4[0:nr, kc % 2, :]),
                waits=[tok, wbf4_rot.get(slot) if kc == 0 else None], inc="sp")
            wld4_rot.set(kc, w4c[kc])
        last_mm = None
        for tch in range(OCHUNK):
            idx = ow * OCHUNK + tch
            ps_off = (idx % 2) * 512
            ps = psF[:, ps_off:ps_off + 512]
            for kc in range(17):
                lhs = (ctx1f[:, kc, tch * 128:(tch + 1) * 128] if kc < 16
                       else ctx1f[0:1, 16, tch * 128:(tch + 1) * 128])
                rhs = wbf4[:, slot, kc, :] if kc < 16 else wbf4[0:1, slot, 16, :]
                last_mm = S.do("tensor", lambda e, lhs=lhs, rhs=rhs, kc=kc, ps=ps:
                               e.matmul(ps, lhs, rhs, start=(kc == 0),
                                        stop=(kc == 16)),
                               waits=[w4c[kc]] +
                                     ([ps10_rot.get(idx), v_ctxf,
                                       v_or] if kc == 0 else []),
                               inc="tp" if kc == 16 else None)
            v_s1d = S.do("vector", lambda e, tch=tch, ow=ow, ps=ps: e.tensor_copy(
                s1[:, tch, ow * 512:(ow + 1) * 512], ps), waits=[last_mm], inc="vp")
            ps10_rot.set(idx, v_s1d)
        wbf4_rot.set(slot, last_mm)
    bar("bar10")
    tap("s1", s1)

    sbuf_free("ctx1", "ctx1f", "wld4", "wbf4")
    b_p10 = S.bnd()
    hq_fm = sbuf("hq_fm", [128, HCH, TOWN], BF16)
    hld = sbuf("hld", [128, 2, 512], F32)
    h8 = sbuf("h8", [128, 2, 512], I8)
    hqb = sbuf("hqb", [128, 2, 512], BF16)
    hld_rot = Rot(2)
    h8_rot = Rot(2)
    hqb_rot = Rot(2)

    for tch in range(OCHUNK):
        for hw in range(11):
            idx = tch * 11 + hw
            tok = S.dma("sync", hld[:, idx % 2, :],
                        h_scr[tch * 128:(tch + 1) * 128, hw * 512:(hw + 1) * 512],
                        waits=[hld_rot.get(idx)] + h_fin +
                              [b_p10[1] if idx == 0 else None],
                        q=f"qs{2 + idx % 2}")
            s_8 = S.do("scalar", lambda e, b=idx % 2, tch=tch: e.activation(
                h8[:, b, :], hld[:, b, :], AF.Identity, scale=scq2[:, tch:tch + 1]),
                waits=[tok, h8_rot.get(idx), v_scq] +
                      (b_p10 if idx == 0 else []), inc="sp")
            hld_rot.set(idx, s_8)
            v_b = S.do("vector", lambda e, b=idx % 2: e.tensor_copy(
                hqb[:, b, :], h8[:, b, :]),
                waits=[s_8, hqb_rot.get(idx)] + (b_p10 if idx == 0 else []),
                inc="vp")
            h8_rot.set(idx, v_b)
            ev, tl = transpose_tile(
                lambda cc, b=idx % 2: hqb[:, b, cc * 128:(cc + 1) * 128],
                lambda hg, nb, tch=tch, hw=hw: AP(
                    hq_fm, (hw * 4) * TOWN + tch * 128,
                    [[HCH * TOWN, 128], [TOWN, nb], [1, 128]]),
                4, v_b)
            hqb_rot.set(idx, tl)
    v_hq = S.now("vp")

    psB16cm.__exit__(None, None, None)
    psG = nc.psum_tensor("psG", [128, 1024], F32).__enter__()

    w2ld = sbuf("w2ld", [128, 4, 512], F32)
    w28 = sbuf("w28", [128, 4, 512], I8)
    NT2 = 8
    tern2 = sbuf("tern2", [128, NT2, 512], BF16)
    t1c = sbuf("t1c", [128, 2, 512], F32)
    t2c = sbuf("t2c", [128, 2, 512], F32)
    x1ld = sbuf("x1ld", [128, 2, 512], F32)
    obuf = sbuf("obuf", [128, 2, 512], F32)
    w2ld_rot = Rot(4)
    w28_rot = Rot(4)
    tern_rot = Rot(NT2)
    ps11_rot = Rot(8)
    x1l_rot = Rot(2)
    ob_rot = Rot(2)
    t1c_rot = Rot(2)

    def ps8ap(tch):
        if tch < 6:
            return psF[:, tch * 512:(tch + 1) * 512]
        return psG[:, (tch - 6) * 512:(tch - 5) * 512]

    NGP = 4 * HCH

    def tern11(gi):
        ow, kc = divmod(gi, HCH)
        tok = S.dma("sync", w2ld[:, gi % 4, :],
                    w2tI[kc * 128:(kc + 1) * 128, ow * 512:(ow + 1) * 512],
                    waits=[w2ld_rot.get(gi), v_hq if gi == 0 else None],
                    q=f"qs{4 + gi % 4}")
        s_t = S.do("scalar", lambda e, b4=gi % 4: e.activation(
            w28[:, b4, :], w2ld[:, b4, :], AF.Identity, scale=mu128[:, 2:3]),
            waits=[tok, w28_rot.get(gi)], inc="sp")
        w2ld_rot.set(gi, s_t)
        v_t = S.do("vector", lambda e, b4=gi % 4, b8=gi % NT2: e.tensor_scalar(
            tern2[:, b8, :], w28[:, b4, :], -1.0, 1.0, ALU.max, ALU.min),
            waits=[s_t, tern_rot.get(gi)], inc="vp")
        w28_rot.set(gi, v_t)
        return v_t

    def evac11(ow, t_fin):
        for tch in range(OCHUNK):
            b = tch % 2
            s_si = S.do("scalar", lambda e, b=b, tch=tch: e.activation(
                t1c[:, b, :], ps8ap(tch), AF.Silu, scale=scg2[:, tch:tch + 1]),
                waits=[t_fin, t1c_rot.get(tch)], inc="sp")
            ps11_rot.set(tch, s_si)
            x1t = S.dma("gpsimd", x1ld[:, b, :],
                        x1_scr[tch * 128:(tch + 1) * 128,
                               ow * 512:(ow + 1) * 512],
                        waits=[x1l_rot.get(tch), spill_all,
                               b_p10[1] if (ow == 0 and tch == 0) else None],
                        q=f"qg{1 + b}")
            S.do("vector", lambda e, b=b, tch=tch, ow=ow: e.tensor_tensor(
                t2c[:, b, :], t1c[:, b, :], s1[:, tch, ow * 512:(ow + 1) * 512],
                ALU.mult), waits=[s_si, v_s1d])
            v_o = S.do("vector", lambda e, b=b: e.tensor_tensor(
                obuf[:, b, :], t2c[:, b, :], x1ld[:, b, :], ALU.add),
                waits=[x1t, ob_rot.get(tch)], inc="vp")
            x1l_rot.set(tch, v_o)
            t1c_rot.set(tch, v_o)
            ot = S.dma("gpsimd", out[tch * 128:(tch + 1) * 128,
                                     ow * 512:(ow + 1) * 512],
                       obuf[:, b, :], waits=[v_o], q=f"qg{8 + b}")
            ob_rot.set(tch, ot)

    v_ts = {gi: tern11(gi) for gi in range(NT2)}
    t_fin = None
    for gp in range(NGP):
        ow, kc = divmod(gp, HCH)
        if kc == 0 and ow > 0:
            evac11(ow - 1, t_fin)
        for tch in range(OCHUNK):
            wl = [v_ts[gp]]
            if kc == 0:
                wl.append(ps11_rot.get(tch))
            t_fin = S.do("tensor", lambda e, kc=kc, tch=tch, b8=gp % NT2:
                         e.matmul(ps8ap(tch),
                                  hq_fm[:, kc, tch * 128:(tch + 1) * 128],
                                  tern2[:, b8, :], start=(kc == 0),
                                  stop=(kc == HCH - 1)),
                         waits=wl if (kc == 0 or tch == 0) else (),
                         inc="tp" if tch == OCHUNK - 1 else None)
        tern_rot.set(gp, t_fin)
        if gp + NT2 < NGP:
            v_ts[gp + NT2] = tern11(gp + NT2)
    evac11(3, t_fin)

    S.do("gpsimd", lambda e: e.nop(),
         waits=[(s, S.cnt[s]) for s in QG_ALL if S.cnt[s] > 0])

    with nc.Block() as block:
        S.finalize(block)
    return nc, S


def _stage_core(inputs, core):
    seq, half = core // 2, core % 2
    f32 = np.float32
    d = {}
    x_seq = np.ascontiguousarray(np.asarray(inputs["L0_attn_x"])[seq], dtype=f32)
    d["x"] = x_seq
    d["x_own"] = np.ascontiguousarray(x_seq[half * 1024:(half + 1) * 1024])
    PERMQ = [0, 4, 1, 5, 2, 6, 3, 7, 8, 12, 9, 13, 10, 14, 11, 15]
    wq_half = np.asarray(inputs["L0_attn_Wq"], f32)[:, half * 1024:(half + 1) * 1024]
    wq_blocks = wq_half.reshape(D, 16, 64)
    d["l0wq"] = np.ascontiguousarray(
        wq_blocks[:, PERMQ, :].reshape(D, 1024))
    d["l0wk"] = np.ascontiguousarray(
        np.asarray(inputs["L0_attn_Wk"], f32)[:, half * 256:(half + 1) * 256])
    d["l0wv"] = np.ascontiguousarray(
        np.asarray(inputs["L0_attn_Wv"], f32)[:, half * 256:(half + 1) * 256])
    d["l0wo"] = np.ascontiguousarray(np.asarray(inputs["L0_attn_Wo"], f32))
    w8 = np.asarray(inputs["L0_ffn_weight"])
    d["l0w8"] = np.ascontiguousarray(w8.T).view(ml_dtypes.float8_e4m3)
    d["w0s"] = np.broadcast_to(
        np.asarray(inputs["L0_ffn_weight_scale"], f32).reshape(1, 1),
        (128, 1)).copy()
    d["in_s"] = np.broadcast_to(
        np.asarray(inputs["L0_ffn_input_scale"]).reshape(1, 1), (128, 1)).copy()
    inv = 1.0 / 10000.0 ** (np.arange(32, dtype=f32) * 2.0 / HD)
    ang = np.arange(T, dtype=f32)[:, None] * inv[None, :]
    d["cosP"] = np.ascontiguousarray(
        np.cos(ang).astype(BF).reshape(16, 128, 32).transpose(1, 0, 2)
        .reshape(128, 512))
    d["sinP"] = np.ascontiguousarray(
        np.sin(ang).astype(BF).reshape(16, 128, 32).transpose(1, 0, 2)
        .reshape(128, 512))
    d["identI"] = np.eye(128, dtype=BF)
    kk = np.arange(128)
    d["tri01"] = (kk[:, None] <= kk[None, :]).astype(BF)
    d["m1"] = d["tri01"] if half == 0 else np.ones((128, 128), BF)
    kk2 = np.arange(128)[:, None]
    ql = np.arange(512)[None, :]
    wm = np.concatenate([(ql >= i * 128 + kk2) for i in range(4)],
                        axis=1).astype(BF)
    d["wmask"] = wm
    d["w1mask"] = wm if half == 0 else np.ones((128, 2048), BF)
    d["zA"] = np.full((128, 1), 0.0 if half == 0 else 1.0, np.float32)
    al = np.zeros((T,), f32)
    al[:(half + 1) * 1024] = 1.0
    d["alive"] = np.ascontiguousarray(al.reshape(16, 128).T)
    hs = np.zeros((128, 2), f32)
    hs[:, half] = 1.0
    d["halfsel"] = hs
    for nm, w, b in (("wqt", "L1_attn_Wq", "L1_attn_bq"),
                     ("wkt", "L1_attn_Wk", "L1_attn_bk"),
                     ("wvt", "L1_attn_Wv", "L1_attn_bv"),
                     ("wot", "L1_attn_Wo", "L1_attn_bo")):
        W = np.asarray(inputs[w], f32)
        bb = np.asarray(inputs[b], f32)
        d[nm] = np.ascontiguousarray(np.concatenate([W.T, bb[None, :]], axis=0))
    d["w1t"] = np.ascontiguousarray(np.asarray(inputs["L1_ffn_w1"], f32).T)
    d["w3t"] = np.ascontiguousarray(np.asarray(inputs["L1_ffn_w3"], f32).T)
    d["w2t"] = np.ascontiguousarray(np.asarray(inputs["L1_ffn_w2"], f32).T)
    d["abs1"] = np.ascontiguousarray(d["w1t"][core * 256:(core + 1) * 256])
    d["abs3"] = np.ascontiguousarray(d["w3t"][core * 256:(core + 1) * 256])
    d["abs2"] = np.ascontiguousarray(d["w2t"][core * 704:(core + 1) * 704])
    return d


_NC_CACHE = {}


def kernel(**inputs):
    if "nc" not in _NC_CACHE:
        _NC_CACHE["nc"] = build()[0]
    nc = _NC_CACHE["nc"]
    in_maps = [_stage_core(inputs, c) for c in range(8)]
    res = run_bass_kernel_spmd(nc, in_maps, core_ids=list(range(8)))
    outp = np.zeros((4, T, D), np.float32)
    for c in range(8):
        seq, half = c // 2, c % 2
        outp[seq, half * 1024:(half + 1) * 1024] = np.asarray(res.results[c]["out"])
    return outp


# version 11 (final)
# speedup vs baseline: 1.5399x; 1.0180x over previous
from contextlib import ExitStack
import numpy as np
import ml_dtypes

import concourse.bass as bass
import concourse.mybir as mybir
from concourse.bass_utils import run_bass_kernel_spmd

F32 = mybir.dt.float32
BF16 = mybir.dt.bfloat16
I8 = mybir.dt.int8
FP8 = mybir.dt.float8e4
AF = mybir.ActivationFunctionType
ALU = mybir.AluOpType
AXX = mybir.AxisListType.X
BF = ml_dtypes.bfloat16
AP = bass.AP

D = 2048
T = 2048
TOWN = 1024
HD = 64
HID = 5632
NCHUNK = 16
OCHUNK = 8
DCH = 16
HCH = 44

ENGINES = ("sync", "tensor", "vector", "scalar", "gpsimd")


class Sched:

    def __init__(self, nc):
        self.nc = nc
        self.q = {e: [] for e in ENGINES}
        self.h = {}
        self.cnt = {}
        self.waited = {e: {} for e in ENGINES}
        self.ctx = ExitStack()

    def sem(self, name):
        self.h[name] = self.ctx.enter_context(self.nc.semaphore(name))
        self.cnt[name] = 0

    AUTOSEM = {"vector": "vp", "scalar": "sp"}

    def do(self, eng, fn, waits=(), inc=None, inc_n=1):
        auto = self.AUTOSEM.get(eng)
        waits = [wt for wt in waits if wt is not None]
        if auto is not None:
            waits.append((auto, self.cnt[auto]))
            assert inc in (None, auto), (eng, inc)
            inc, inc_n = auto, 1
        for s, v in waits:
            assert v <= self.cnt[s], (eng, s, v, self.cnt[s], "wait on future")
        w = []
        for s, v in waits:
            if v > self.waited[eng].get(s, 0):
                w.append((self.h[s], v))
                self.waited[eng][s] = v
        i = None
        if inc is not None:
            self.cnt[inc] += inc_n
            i = (self.h[inc], inc_n)
        self.q[eng].append((fn, w, i))
        return (inc, self.cnt[inc]) if inc is not None else None

    def dma(self, eng, out, in_, waits=(), q="qs"):
        return self.do(eng, lambda e, o=out, i=in_: e.dma_start(o, i),
                       waits=waits, inc=q, inc_n=16)

    def now(self, sem):
        return (sem, self.cnt[sem])

    def bnd(self):
        return ([self.now("vp"), self.now("tp"), self.now("sp")] +
                [self.now(s) for s in self.cnt if s.startswith("qg")])

    def finalize(self, block):
        for ename in ENGINES:
            ops = self.q[ename]

            def body(eng, ops=ops):
                for fn, w, i in ops:
                    for hh, v in w:
                        eng.wait_ge(hh, v)
                    inst = fn(eng)
                    if i is not None:
                        inst.then_inc(i[0], i[1])
            getattr(block, ename)(body)


class Rot:
    def __init__(self, n):
        self.t = [None] * n
        self.n = n

    def get(self, i):
        return self.t[i % self.n]

    def set(self, i, tok):
        self.t[i % self.n] = tok


def build(taps=()):
    nc = bass.Bass("TRN2", target_bir_lowering=False,
                   detect_race_conditions=False)

    EI = lambda n, s, dt: nc.dram_tensor(n, s, dt, kind="ExternalInput")
    x_in = EI("x", [T, D], F32)
    x_own = EI("x_own", [TOWN, D], F32)
    l0wq = EI("l0wq", [D, 1024], F32)
    l0wk = EI("l0wk", [D, 256], F32)
    l0wv = EI("l0wv", [D, 256], F32)
    l0wo = EI("l0wo", [D, D], F32)
    l0w8 = EI("l0w8", [D, D], FP8)
    w0sI = EI("w0s", [128, 1], F32)
    in_sI = EI("in_s", [128, 1], BF16)
    cosI = EI("cosP", [128, 512], BF16)
    sinI = EI("sinP", [128, 512], BF16)
    identI = EI("identI", [128, 128], BF16)
    tri01I = EI("tri01", [128, 128], BF16)
    m1I = EI("m1", [128, 128], BF16)
    wmI = EI("wmask", [128, 2048], BF16)
    w1mI = EI("w1mask", [128, 2048], BF16)
    zAI = EI("zA", [128, 1], F32)
    aliveI = EI("alive", [128, 16], F32)
    halfI = EI("halfsel", [128, 2], F32)
    wqtI = EI("wqt", [D + 1, D], F32)
    wktI = EI("wkt", [D + 1, D], F32)
    wvtI = EI("wvt", [D + 1, D], F32)
    wotI = EI("wot", [D + 1, D], F32)
    w1tI = EI("w1t", [D, HID], F32)
    w3tI = EI("w3t", [D, HID], F32)
    w2tI = EI("w2t", [HID, D], F32)
    abs1I = EI("abs1", [256, HID], F32)
    abs3I = EI("abs3", [256, HID], F32)
    abs2I = EI("abs2", [704, D], F32)

    out = nc.dram_tensor("out", [TOWN, D], F32, kind="ExternalOutput")

    h_scr = nc.dram_tensor("h_scr", [TOWN, HID], F32)
    x1_scr = nc.dram_tensor("x1_scr", [TOWN, D], F32)
    ag_in = nc.dram_tensor("ag_in", [TOWN, 2 * D], BF16)
    ag_out = nc.dram_tensor("ag_out", [T, 2 * D], BF16)
    ar_in = nc.dram_tensor("ar_in", [1, 4], F32)
    ar_out = nc.dram_tensor("ar_out", [1, 4], F32)

    QG_ALL = ("qg", "qg0", "qg1", "qg2", "qg3", "qg4", "qg5", "qg6", "qg7",
              "qg8", "qg9")
    S = Sched(nc)
    for s in ("qs", "qs0", "qs1", "qs2", "qs3", "qs4", "qs5", "qs6", "qs7",
              "tp", "vp", "sp", "cc") + QG_ALL:
        S.sem(s)
    ctx = S.ctx
    sbent = lambda n, s, dt: ctx.enter_context(nc.sbuf_tensor(n, s, dt))

    class Arena:

        def __init__(self, base, size):
            self.base, self.size = base, size
            self.freelist = [(0, size)]
            self.amap = {}
            self.peak = 0

        def alloc(self, name, shape, dt):
            per = mybir.dt.size(dt)
            for sdim in shape[1:]:
                per *= sdim
            per = (per + 31) // 32 * 32
            for i, (o, s) in enumerate(self.freelist):
                if s >= per:
                    self.freelist[i] = (o + per, s - per)
                    self.amap[name] = (o, per)
                    self.peak = max(self.peak, o + per)
                    return nc.alloc_sbuf_tensor_at(name, shape, dt,
                                                   offset=self.base + o)
            raise MemoryError((name, per, self.freelist))

        def release(self, *names):
            for n in names:
                o, s = self.amap.pop(n)
                self.freelist.append((o, s))
            self.freelist.sort()
            merged = []
            for o, s in self.freelist:
                if merged and merged[-1][0] + merged[-1][1] == o:
                    merged[-1] = (merged[-1][0], merged[-1][1] + s)
                else:
                    merged.append((o, s))
            self.freelist = merged

    arena_ref = []

    def sbuf(name, shape, dt):
        return arena_ref[0].alloc(name, shape, dt)

    def sbuf_free(*names):
        arena_ref[0].release(*names)

    ident = sbent("identc", [128, 128], BF16)
    tri01 = sbent("tri01c", [128, 128], BF16)
    m1 = sbent("m1c", [128, 128], BF16)
    wmk = sbent("wmk", [128, 2048], BF16)
    w1mk = sbent("w1mk", [128, 2048], BF16)
    zA = sbent("zAc", [128, 1], F32)
    alive = sbent("alivec", [128, 16], F32)
    cosT = sbent("cosTc", [128, 16, 32], BF16)
    sinT = sbent("sinTc", [128, 16, 32], BF16)
    halfc = sbent("halfc", [128, 2], F32)
    onesbf = sbent("onesbf", [1, 128], BF16)
    onesf = sbent("onesf", [1, 128], F32)
    onescf = sbent("onescf", [128, 1], F32)
    eighth = sbent("eighth", [128, 1], F32)
    epsT = sbent("epsT", [128, 1], F32)
    sc_misc = sbent("sc_misc", [128, 8], F32)
    mu128 = sbent("mu128", [128, 8], F32)
    qn2acc = sbent("qn2acc", [128, 32], F32)
    kn2acc = sbent("kn2acc", [128, 32], F32)
    cterm0 = sbent("cterm0", [128, 16], F32)
    cterm1 = sbent("cterm1", [128, 32], F32)
    statT = sbent("statT", [128, 2, 40], F32)
    rall = sbent("rall", [128, 8], F32)
    sc1a = sbent("sc1a", [128, 8], F32)
    sc3a = sbent("sc3a", [128, 8], F32)
    am2 = sbent("am2", [128, 8], F32)
    scq2 = sbent("scq2", [128, 8], F32)
    scg2 = sbent("scg2", [128, 8], F32)
    rowscr = sbent("rowscr", [1, 64], F32)
    rowbf = sbent("rowbf", [1, 64], BF16)
    colscr = sbent("colscr", [128, 8], F32)
    colbf = sbent("colbf", [128, 32], BF16)
    sbf16 = sbent("sbf16", [128, 1], BF16)

    psF = ctx.enter_context(nc.psum_tensor("psF", [128, 3072], F32))
    psB16cm = nc.psum_tensor("psB16", [128, 2048], BF16)
    psB16 = psB16cm.__enter__()

    _abase = (nc._sbuf_addr_for_side("left") + 31) // 32 * 32
    _asize = (nc.sbuf_bytes_remaining - 64) // 32 * 32
    arena_ref.append(Arena(_abase, _asize))

    TAPS = []

    def tap(name, hh):
        if name in taps:
            flat = 1
            for sdim in hh.shape[1:]:
                flat *= sdim
            tdr = nc.dram_tensor("tap_" + name, [128, flat], hh.dtype,
                                 kind="ExternalOutput")
            S.dma("gpsimd", tdr[:], AP(hh, 0, [[flat, 128], [1, flat]]),
                  waits=S.bnd(), q="qg")

    def bar(name):
        if name in taps:
            S.do("gpsimd", lambda e: e.nop(), waits=S.bnd())

    absacc = sbuf("absacc", [128, 4], F32)
    bigbuf = sbuf("bigbuf", [128, 2, 2816], F32)

    S.dma("sync", ident[:], identI[:])
    S.dma("sync", tri01[:], tri01I[:])
    S.dma("sync", m1[:], m1I[:])
    S.dma("sync", wmk[:], wmI[:])
    S.dma("sync", w1mk[:], w1mI[:])
    S.dma("sync", zA[:], zAI[:])
    S.dma("sync", alive[:], aliveI[:])
    S.dma("sync", halfc[:], halfI[:])
    S.dma("sync", cosT[:], cosI.ap().rearrange("p (c k) -> p c k", k=32))
    S.dma("sync", sinT[:], sinI.ap().rearrange("p (c k) -> p c k", k=32))
    S.dma("sync", sc_misc[:, 2:3], w0sI[:])
    t_con = S.dma("sync", sbf16[:], in_sI[:])

    S.do("vector", lambda e: e.memset(onesbf[:], 1.0))
    S.do("vector", lambda e: e.memset(onesf[:], 1.0))
    S.do("vector", lambda e: e.memset(onescf[:], 1.0))
    S.do("vector", lambda e: e.memset(eighth[:], 0.125))
    S.do("vector", lambda e: e.memset(epsT[:], 1e-5))
    S.do("vector", lambda e: e.memset(absacc[:], 0.0))
    S.do("vector", lambda e: e.tensor_copy(colscr[:, 0:1], sbf16[:]), waits=[t_con])
    S.do("vector", lambda e: e.reciprocal(sc_misc[:, 0:1], colscr[:, 0:1]))
    v_consts = S.do("vector", lambda e: e.tensor_tensor(
        sc_misc[:, 1:2], colscr[:, 0:1], sc_misc[:, 2:3], ALU.mult), inc="vp")

    bigrot = Rot(2)

    def abs_accum(src, nrows, ncols, accol, idx0):
        idx = idx0
        nch = (nrows + 127) // 128
        nj = (ncols + 2815) // 2816
        for c in range(nch):
            pr = min(128, nrows - c * 128)
            for j in range(nj):
                w = min(2816, ncols - j * 2816)
                tok = S.dma("sync", bigbuf[0:pr, idx % 2, 0:w],
                            src[c * 128:c * 128 + pr, j * 2816:j * 2816 + w],
                            waits=[bigrot.get(idx), v_consts], q=f"qs{idx % 2}")
                S.do("vector", lambda e, pr=pr, b=idx % 2, w=w: e.reduce_sum(
                    colscr[0:pr, 4:5], bigbuf[0:pr, b, 0:w], axis=AXX,
                    apply_absolute_value=True), waits=[tok])
                bigrot.set(idx, S.do("vector", lambda e, pr=pr, a=accol: e.tensor_tensor(
                    absacc[0:pr, a:a + 1], absacc[0:pr, a:a + 1],
                    colscr[0:pr, 4:5], ALU.add), inc="vp"))
                idx += 1
        return idx

    nidx = abs_accum(abs1I, 256, HID, 0, 0)
    nidx = abs_accum(abs3I, 256, HID, 1, nidx)
    nidx = abs_accum(abs2I, 704, D, 2, nidx)
    t_sum = S.do("tensor", lambda e: e.matmul(psF[0:1, 0:4], onescf[:], absacc[:],
                                              start=True, stop=True),
                 waits=[S.now("vp")], inc="tp")
    v_sum = S.do("vector", lambda e: e.tensor_copy(colscr[0:1, 0:4], psF[0:1, 0:4]),
                 waits=[t_sum], inc="vp")
    g_ar = S.dma("gpsimd", ar_in[:], colscr[0:1, 0:4], waits=[v_sum], q="qg0")
    cc_mu = S.do("gpsimd", lambda e: e.collective_compute(
        "AllReduce", ALU.add, replica_groups=[list(range(8))],
        ins=[ar_in.ap().opt()], outs=[ar_out.ap().opt()]),
        waits=[g_ar], inc="cc")
    t_muld = S.dma("sync", rowscr[0:1, 0:4], ar_out[:], waits=[cc_mu])
    S.do("vector", lambda e: e.tensor_scalar(rowscr[0:1, 4:6], rowscr[0:1, 0:2],
                                             1.0 / (HID * D), 1e-5, ALU.mult, ALU.max),
         waits=[t_muld])
    S.do("vector", lambda e: e.tensor_scalar(rowscr[0:1, 6:7], rowscr[0:1, 2:3],
                                             1.0 / (HID * D), 1e-5, ALU.mult, ALU.max))
    v_mu = S.do("vector", lambda e: e.reciprocal(rowscr[0:1, 8:11], rowscr[0:1, 4:7]),
                inc="vp")
    S.do("tensor", lambda e: e.matmul(psF[:, 4:7], onesf[:], rowscr[0:1, 8:11],
                                      start=True, stop=True), waits=[v_mu])
    t_mu2 = S.do("tensor", lambda e: e.matmul(psF[:, 8:11], onesf[:], rowscr[0:1, 4:7],
                                              start=True, stop=True), inc="tp")
    S.do("vector", lambda e: e.tensor_copy(mu128[:, 0:3], psF[:, 4:7]), waits=[t_mu2])
    v_mu128 = S.do("vector", lambda e: e.tensor_copy(mu128[:, 4:7], psF[:, 8:11]),
                   inc="vp")

    TRC = [0]
    psrot = Rot(2)

    def transpose_tile(srcf, dstf, nblk, wait_first):
        lastev = lasttp = None
        nhg = (nblk + 7) // 8
        for hg in range(nhg):
            nb = min(8, nblk - hg * 8)
            key = TRC[0]
            TRC[0] += 1
            slot = key % 2
            wl = [wait_first, psrot.get(key)]
            tptok = None
            for c in range(nb):
                cc = hg * 8 + c
                tptok = S.do("tensor", lambda e, cc=cc, c=c, slot=slot, nb=nb:
                             e.matmul(
                                 psB16[:, slot * 1024 + c * 128:slot * 1024 + (c + 1) * 128],
                                 srcf(cc), ident[:], start=(c == 0),
                                 stop=(c == nb - 1), is_transpose=True),
                             waits=wl if c == 0 else (),
                             inc="tp" if c == nb - 1 else None)
            ev = S.do("vector", lambda e, hg=hg, nb=nb, slot=slot: e.tensor_copy(
                dstf(hg, nb),
                AP(psB16, slot * 1024, [[2048, 128], [128, nb], [1, 128]])),
                waits=[tptok], inc="vp")
            psrot.set(key, ev)
            lastev, lasttp = ev, tptok
        return lastev, lasttp

    ln_guard = Rot(2)

    def ln_stats(src_ap, stat_wait, slot):
        st = statT[:, slot, :]
        for c in range(4):
            S.do("vector", lambda e, s=src_ap, st=st, c=c: e.bn_stats(
                st[:, 6 * c:6 * c + 6], s[:, 512 * c:512 * (c + 1)]),
                waits=(stat_wait + [ln_guard.get(slot)]) if c == 0 else ())
        v_ag = S.do("vector", lambda e, st=st: e.bn_aggr(st[:, 24:26],
                                                         st[:, 0:24]),
                    inc="vp")
        s_sd = S.do("scalar", lambda e, st=st: e.activation(
            st[:, 26:27], st[:, 25:26], AF.Sqrt, bias=epsT[:]),
            waits=[v_ag], inc="sp")
        S.do("vector", lambda e, st=st: e.reciprocal(st[:, 27:28], st[:, 26:27]),
             waits=[s_sd])
        return S.do("vector", lambda e, st=st: e.scalar_tensor_tensor(
            st[:, 28:29], st[:, 24:25], -1.0, st[:, 27:28], ALU.mult, ALU.mult),
            inc="vp")

    def cterm_pipe(qt, qc0, kap, dst, dc0, nh, vguard):
        v0 = S.do("vector", lambda e, nh=nh: e.tensor_copy(
            colbf[:, 0:nh], qt[:, qc0:qc0 + nh]), waits=vguard, inc="vp")
        t0 = S.do("tensor", lambda e, nh=nh: e.transpose(
            psB16[0:nh, 0:128], colbf[:, 0:nh], ident[:]),
            waits=[v0, psrot.get(TRC[0])], inc="tp")
        v1 = S.do("vector", lambda e, nh=nh: e.reduce_max(
            colscr[0:nh, 0:1], psB16[0:nh, 0:128], axis=AXX), waits=[t0], inc="vp")
        v2 = S.do("vector", lambda e, nh=nh, kap=kap: e.tensor_copy(
            colbf[:, 0:nh], kap), waits=[v1], inc="vp")
        t1 = S.do("tensor", lambda e, nh=nh: e.transpose(
            psB16[0:nh, 0:128], colbf[:, 0:nh], ident[:]), waits=[v2], inc="tp")
        S.do("vector", lambda e, nh=nh: e.reduce_max(
            colscr[0:nh, 1:2], psB16[0:nh, 0:128], axis=AXX), waits=[t1])
        v3 = S.do("vector", lambda e, nh=nh: e.tensor_tensor(
            colscr[0:nh, 2:3], colscr[0:nh, 0:1], colscr[0:nh, 1:2], ALU.mult),
            inc="vp")
        s0 = S.do("scalar", lambda e, nh=nh: e.activation(
            colscr[0:nh, 3:4], colscr[0:nh, 2:3], AF.Sqrt), waits=[v3], inc="sp")
        S.do("vector", lambda e, nh=nh: e.tensor_scalar(
            colscr[0:nh, 4:5], colscr[0:nh, 3:4], -0.125, None, ALU.mult),
            waits=[s0])
        v4 = S.do("vector", lambda e, nh=nh: e.tensor_copy(
            colbf[0:nh, 16:17], colscr[0:nh, 4:5]), inc="vp")
        t2 = S.do("tensor", lambda e, nh=nh: e.transpose(
            psB16[0:1, 0:nh], colbf[0:nh, 16:17], ident[0:nh, 0:nh]),
            waits=[v4], inc="tp")
        v5 = S.do("vector", lambda e, nh=nh: e.tensor_copy(
            rowbf[0:1, 0:nh], psB16[0:1, 0:nh]), waits=[t2], inc="vp")
        t3 = S.do("tensor", lambda e, nh=nh: e.matmul(
            psF[:, 2560:2560 + nh], onesbf[:], rowbf[0:1, 0:nh], start=True,
            stop=True), waits=[v5], inc="tp")
        return S.do("vector", lambda e, nh=nh, dc0=dc0, dst=dst: e.tensor_copy(
            dst[:, dc0:dc0 + nh], psF[:, 2560:2560 + nh]), waits=[t3], inc="vp")

    def attention(nheads, nqw, nkc, lhsK, rhsQ, rhsV, biasAP, mask_list, evac_ap,
                  first_wait, PtT, rcpT):
        sc_rot = Rot(2)
        pt_rot = Rot(2)
        pc_rot = Rot(2)
        items = []
        for h in range(nheads):
            for qw in range(nqw):
                nk = nkc(qw)
                ng = (nk + 1) // 2
                for g in range(ng):
                    items.append((h, qw, g, nk, ng))

        def kcs_of(it):
            h, qw, g, nk, ng = it
            return [k for k in (2 * g, 2 * g + 1) if k < nk]

        def emit_scores(gidx, it):
            h, qw, g, nk, ng = it
            kcs = kcs_of(it)
            sc_off = 1024 * (gidx % 2)
            wl = [sc_rot.get(gidx)] + (list(first_wait) if gidx == 0 else [])
            ttok = None
            for ii, kc in enumerate(kcs):
                ttok = S.do("tensor", lambda e, ii=ii, kc=kc, sc_off=sc_off,
                            h=h, qw=qw: e.matmul(
                                psF[:, sc_off + ii * 512:sc_off + (ii + 1) * 512],
                                lhsK(h, kc), rhsQ(h, qw), start=True,
                                stop=True),
                            waits=wl if ii == 0 else (),
                            inc="tp" if ii == len(kcs) - 1 else None)
            return ttok

        def emit_expmask(gidx, it, ttok):
            h, qw, g, nk, ng = it
            kcs = kcs_of(it)
            sc_off = 1024 * (gidx % 2)
            wl2 = [ttok, pt_rot.get(gidx)] + (
                list(first_wait) if gidx == 0 else [])
            s_exp = S.do("scalar", lambda e, g2=gidx % 2, sc_off=sc_off, h=h,
                         nn=len(kcs) * 512, Pt=PtT: e.activation(
                             Pt[:, g2, 0:nn], psF[:, sc_off:sc_off + nn],
                             AF.Exp, scale=eighth[:], bias=biasAP(h)),
                         waits=wl2, inc="sp")
            sc_rot.set(gidx, s_exp)
            mops = [mask_list(qw, kc) for kc in kcs]
            mtok = [None] * len(kcs)

            def emit_mask(iis, fn):
                t = S.do("vector", fn, waits=[s_exp], inc="vp")
                for i3 in iis:
                    mtok[i3] = t

            i4 = 0
            while i4 < len(kcs):
                ml = mops[i4]
                if (len(kcs) == 2 and i4 == 0 and len(ml) == 1 and
                        len(mops[1]) == 1 and ml[0][0] == mops[1][0][0]):
                    a, b2 = ml[0], mops[1][0]
                    if (a[0] == "tt" and a[1] is b2[1] and
                            b2[2] == a[2] + 512):
                        emit_mask([0, 1], lambda e, g2=gidx % 2, mt=a[1],
                                  c0=a[2], Pt=PtT: e.tensor_tensor(
                                      Pt[:, g2, 0:1024], Pt[:, g2, 0:1024],
                                      mt[:, c0:c0 + 1024], ALU.mult))
                        i4 = 2
                        continue
                    if a[0] == "ts" and b2[0] == "ts" and a[1] is b2[1]:
                        emit_mask([0, 1], lambda e, g2=gidx % 2, sap=a[1],
                                  Pt=PtT: e.tensor_scalar(
                                      Pt[:, g2, 0:1024], Pt[:, g2, 0:1024],
                                      sap, None, ALU.mult))
                        i4 = 2
                        continue
                for mop in ml:
                    if mop[0] == "tt":
                        emit_mask([i4], lambda e, g2=gidx % 2, ii=i4,
                                  mt=mop[1], c0=mop[2], Pt=PtT:
                                  e.tensor_tensor(
                                      Pt[:, g2, ii * 512:(ii + 1) * 512],
                                      Pt[:, g2, ii * 512:(ii + 1) * 512],
                                      mt[:, c0:c0 + 512], ALU.mult))
                    else:
                        emit_mask([i4], lambda e, g2=gidx % 2, ii=i4,
                                  sap=mop[1], Pt=PtT: e.tensor_scalar(
                                      Pt[:, g2, ii * 512:(ii + 1) * 512],
                                      Pt[:, g2, ii * 512:(ii + 1) * 512],
                                      sap, None, ALU.mult))
                i4 += 1
            return s_exp, mtok

        def emit_ctx(gidx, it, s_exp, mtok):
            h, qw, g, nk, ng = it
            kcs = kcs_of(it)
            psC_off = 2048 + 512 * ((h * nqw + qw) % 2)
            pcw = pc_rot.get(h * nqw + qw) if g == 0 else None
            ttok2 = None
            for ii, kc in enumerate(kcs):
                wl3 = [s_exp, mtok[ii]] + ([pcw] if ii == 0 else [])
                for j in range(4):
                    last = (kc == nk - 1 and j == 3)
                    ttok2 = S.do("tensor", lambda e, g2=gidx % 2, ii=ii,
                                 kc=kc, j=j, psC_off=psC_off, h=h, nk=nk,
                                 Pt=PtT: e.matmul(
                                     psF[:, psC_off + j * 65:
                                         psC_off + (j + 1) * 65],
                                     Pt[:, g2, ii * 512 + j * 128:
                                        ii * 512 + (j + 1) * 128],
                                     rhsV(h, kc),
                                     start=(kc == 0 and ii == 0 and j == 0),
                                     stop=(kc == nk - 1 and j == 3)),
                                 waits=wl3 if j == 0 else (),
                                 inc="tp" if last else None)
            pt_rot.set(gidx, ttok2)
            if g == ng - 1:
                S.do("vector", lambda e, psC_off=psC_off, rcp4=rcpT:
                     e.reciprocal(rcp4[:, 0:4],
                                  AP(psF, psC_off + 64, [[3072, 128], [65, 4]])),
                     waits=[ttok2])
                pc_rot.set(h * nqw + qw + 2,
                           S.do("vector", lambda e, psC_off=psC_off, h=h, qw=qw,
                                rcp4=rcpT: e.tensor_tensor(
                                    evac_ap(h, qw),
                                    AP(psF, psC_off, [[3072, 128], [65, 4], [1, 64]]),
                                    AP(rcp4, 0, [[8, 128], [1, 4], [0, 64]]),
                                    ALU.mult), inc="vp"))

        st = emit_scores(0, items[0])
        for i, it in enumerate(items):
            em = emit_expmask(i, it, st)
            if i + 1 < len(items):
                st = emit_scores(i + 1, items[i + 1])
            emit_ctx(i, it, *em)

    xn_fm = sbuf("xn_fm", [128, DCH, T], BF16)
    xq0_fm = sbuf("xq0_fm", [128, DCH, TOWN], BF16)
    xldb = sbuf("xldb", [128, 4, D], F32)
    xn_tm = sbuf("xn_tm", [128, 2, D], F32)
    xnb_tm = sbuf("xnb_tm", [128, 2, D], BF16)
    q8 = sbuf("q8t", [128, 2, D], FP8)
    q8b = sbuf("q8b", [128, 2, D], BF16)

    v_zero = S.do("vector", lambda e: e.memset(xq0_fm[:], 0.0),
                  waits=[v_consts], inc="vp")
    xn_rot = Rot(2)
    xn2_rot = Rot(2)
    xnb_rot = Rot(2)
    q8_rot = Rot(2)
    q8b_rot = Rot(2)

    x_tok = [None] * NCHUNK
    for i in range(4):
        x_tok[i] = S.dma("sync", xldb[:, i, :], x_in[i * 128:(i + 1) * 128, :],
                         waits=[v_consts], q=f"qs{2 + i % 2}")

    def p1_stats(i):
        return ln_stats(xldb[:, i % 4, :], [x_tok[i]], i % 2)

    def p1_consume(i, v_st):
        b = i % 4
        src = xldb[:, b, :]
        s_xn = S.do("scalar", lambda e, src=src, sl=i % 2: e.activation(
            xn_tm[:, sl, :], src, AF.Identity, scale=statT[:, sl, 27:28],
            bias=statT[:, sl, 28:29]),
            waits=[v_st, xn_rot.get(i), xn2_rot.get(i)], inc="sp")
        ln_guard.set(i, s_xn)
        if i + 4 < NCHUNK:
            x_tok[i + 4] = S.dma("sync", xldb[:, b, :],
                                 x_in[(i + 4) * 128:(i + 5) * 128, :],
                                 waits=[s_xn], q=f"qs{2 + b % 2}")
        v_cast = S.do("vector", lambda e, b2=i % 2: e.tensor_copy(
            xnb_tm[:, b2, :], xn_tm[:, b2, :]),
            waits=[s_xn, xnb_rot.get(i)], inc="vp")
        xn_rot.set(i, v_cast)
        ev, tl = transpose_tile(
            lambda cc, b2=i % 2: xnb_tm[:, b2, cc * 128:(cc + 1) * 128],
            lambda hg, nb, i=i: AP(xn_fm, hg * 8 * T + i * 128,
                                   [[DCH * T, 128], [T, nb], [1, 128]]),
            DCH, v_cast)
        xnb_rot.set(i, tl)
        s_q8 = S.do("scalar", lambda e, b2=i % 2: e.activation(
            q8[:, b2, :], xn_tm[:, b2, :], AF.Identity, scale=sc_misc[:, 0:1]),
            waits=[s_xn, q8_rot.get(i)], inc="sp")
        xn2_rot.set(i, s_q8)
        selcol = 0 if i < 8 else 1
        v_q8 = S.do("vector", lambda e, b2=i % 2, sc=selcol: e.tensor_scalar(
            q8b[:, b2, :], q8[:, b2, :], halfc[:, sc:sc + 1], None, ALU.mult),
            waits=[s_q8, q8b_rot.get(i)], inc="vp")
        q8_rot.set(i, v_q8)
        loc = i % 8
        lasttp = None
        for hg in range(2):
            key = TRC[0]
            TRC[0] += 1
            slot = key % 2
            tptok = None
            for c in range(8):
                cc = hg * 8 + c
                tptok = S.do("tensor", lambda e, b2=i % 2, cc=cc, c=c, slot=slot:
                             e.transpose(
                                 psB16[:, slot * 1024 + c * 128:slot * 1024 + (c + 1) * 128],
                                 q8b[:, b2, cc * 128:(cc + 1) * 128], ident[:]),
                             waits=[v_q8, psrot.get(key)] if c == 0 else (),
                             inc="tp" if c == 7 else None)
            ev2 = S.do("vector", lambda e, loc=loc, hg=hg, slot=slot: e.tensor_tensor(
                AP(xq0_fm, hg * 8 * TOWN + loc * 128,
                   [[DCH * TOWN, 128], [TOWN, 8], [1, 128]]),
                AP(xq0_fm, hg * 8 * TOWN + loc * 128,
                   [[DCH * TOWN, 128], [TOWN, 8], [1, 128]]),
                AP(psB16, slot * 1024, [[2048, 128], [128, 8], [1, 128]]),
                ALU.add), waits=[tptok, v_zero], inc="vp")
            psrot.set(key, ev2)
            lasttp = tptok
        q8b_rot.set(i, lasttp)

    vst_p1 = p1_stats(0)
    for i in range(NCHUNK):
        nxt = p1_stats(i + 1) if i + 1 < NCHUNK else None
        p1_consume(i, vst_p1)
        vst_p1 = nxt
    v_p1 = S.now("vp")
    bar("bar1")
    tap("xn_fm", xn_fm)
    tap("xq0_fm", xq0_fm)
    sbuf_free("xldb", "xn_tm", "xnb_tm", "q8t", "q8b",
               "absacc", "bigbuf")
    b_p1 = S.bnd()

    Q_fm = sbuf("Q_fm", [128, 8, T], BF16)
    K_fm = sbuf("K_fm", [128, 2, T], BF16)
    V_st = sbuf("V_st", [128, NCHUNK, 260], BF16)
    wld = sbuf("wld", [128, 2, 512], F32)
    wbf = sbuf("wbf", [128, 2, DCH, 512], BF16)
    ptm = sbuf("ptm", [128, 2, 512], BF16)
    prp = sbuf("prp", [128, 2, 512], BF16)
    rsc = sbuf("rsc", [128, 2, 512], BF16)
    sqv = sbuf("sqv", [128, 512], BF16)
    qn2t = sbuf("qn2t", [128, 16], F32)

    wld_rot = Rot(2)
    wbf_rot = Rot(2)
    pj_rot = Rot(4)
    ptm_rot = Rot(2)
    prp_rot = Rot(2)

    def stream_w(wsrc, nk, col0, width, slot, extra=None, ldrot=None, bfrot=None,
                 ldbuf=None, bfbuf=None, rows128=True):
        ldrot = ldrot or wld_rot
        bfrot = bfrot or wbf_rot
        ldbuf = ldbuf if ldbuf is not None else wld
        bfbuf = bfbuf if bfbuf is not None else wbf
        wtoks = {}
        for kc in range(nk):
            nr = 128 if (rows128 or kc < nk - 1) else 1
            tok = S.dma("sync", ldbuf[0:nr, kc % 2, 0:width],
                        wsrc[kc * 128:kc * 128 + nr, col0:col0 + width],
                        waits=[ldrot.get(kc), extra if kc == 0 else None],
                        q=f"qs{4 + kc % 2}")
            wtoks[kc] = S.do("scalar", lambda e, kc=kc, nr=nr, w=width, slot=slot,
                             bfbuf=bfbuf, ldbuf=ldbuf: e.copy(
                                 bfbuf[0:nr, slot, kc, 0:w],
                                 ldbuf[0:nr, kc % 2, 0:w]),
                             waits=[tok, bfrot.get(slot) if kc == 0 else None],
                             inc="sp")
            ldrot.set(kc, wtoks[kc])
        return wtoks

    v_ones = None
    for tch in range(NCHUNK):
        v_ones = S.do("vector", lambda e, tch=tch: e.memset(
            AP(V_st, tch * 260 + 64, [[NCHUNK * 260, 128], [65, 4]]), 1.0), inc="vp")

    def rope_block(b, nh4, tch, v_in):
        x1p = AP(ptm, b * 512, [[1024, 128], [64, nh4], [1, 32]])
        x2p = AP(ptm, b * 512 + 32, [[1024, 128], [64, nh4], [1, 32]])
        o1p = AP(prp, b * 512, [[1024, 128], [64, nh4], [1, 32]])
        o2p = AP(prp, b * 512 + 32, [[1024, 128], [64, nh4], [1, 32]])
        cosb = AP(cosT, tch * 32, [[512, 128], [0, nh4], [1, 32]])
        sinb = AP(sinT, tch * 32, [[512, 128], [0, nh4], [1, 32]])
        sA = AP(rsc, b * 512, [[1024, 128], [32, nh4], [1, 32]])
        sB = AP(rsc, b * 512 + nh4 * 32, [[1024, 128], [32, nh4], [1, 32]])
        S.do("vector", lambda e, a=sA, x=x1p, c=cosb: e.tensor_tensor(a, x, c,
                                                                      ALU.mult),
             waits=[v_in, prp_rot.get(tch)])
        S.do("vector", lambda e, a=sB, x=x2p, c=sinb: e.tensor_tensor(a, x, c,
                                                                      ALU.mult))
        S.do("vector", lambda e, o=o1p, a=sA, bq=sB: e.tensor_tensor(o, a, bq,
                                                                     ALU.subtract))
        S.do("vector", lambda e, a=sA, x=x2p, c=cosb: e.tensor_tensor(a, x, c,
                                                                      ALU.mult))
        S.do("vector", lambda e, a=sB, x=x1p, c=sinb: e.tensor_tensor(a, x, c,
                                                                      ALU.mult))
        return S.do("vector", lambda e, o=o2p, a=sA, bq=sB: e.tensor_tensor(
            o, a, bq, ALU.add), inc="vp")

    pidx = 0
    for proj, wsrc, width, ews in (("q", l0wq, 512, 2), ("k", l0wk, 256, 1),
                                   ("v", l0wv, 256, 1)):
        for ew in range(ews):
            slot = pidx % 2
            v_w = stream_w(wsrc, DCH, ew * width, width, slot,
                           extra=b_p1[1] if pidx == 0 else None)
            last_mm_slot = None
            for tch in range(NCHUNK):
                idx = pidx * NCHUNK + tch
                ps_off = (idx % 4) * 512
                ps = psF[:, ps_off:ps_off + width]
                for kc in range(DCH):
                    last_mm_slot = S.do("tensor", lambda e, kc=kc, tch=tch, slot=slot,
                                        ps=ps, width=width: e.matmul(
                                            ps, xn_fm[:, kc, tch * 128:(tch + 1) * 128],
                                            wbf[:, slot, kc, 0:width],
                                            start=(kc == 0), stop=(kc == DCH - 1)),
                                        waits=[v_w[kc]] +
                                              ([pj_rot.get(idx), v_p1]
                                               if kc == 0 else []),
                                        inc="tp" if kc == DCH - 1 else None)
                b = tch % 2
                if proj in ("q", "k"):
                    nh4 = width // 64
                    v_e = S.do("vector", lambda e, b=b, ps=ps, width=width:
                               e.tensor_copy(ptm[:, b, 0:width], ps),
                               waits=[last_mm_slot, ptm_rot.get(tch)], inc="vp")
                    pj_rot.set(idx, v_e)
                    v_r = rope_block(b, nh4, tch, v_e)
                    ptm_rot.set(tch, v_r)
                    S.do("vector", lambda e, b=b, width=width: e.tensor_tensor(
                        sqv[:, 0:width], prp[:, b, 0:width], prp[:, b, 0:width],
                        ALU.mult), waits=[v_r])
                    if proj == "q":
                        red_ap = AP(sqv, 0, [[512, 128], [64, nh4], [1, 64]])
                        nred, a0 = nh4, ew * nh4
                        accd = qn2acc
                    else:
                        red_ap = AP(sqv, 0, [[512, 128], [64, 4], [0, 4], [1, 64]])
                        nred, a0 = 16, 0
                        accd = kn2acc
                    S.do("vector", lambda e, red_ap=red_ap, nred=nred: e.reduce_sum(
                        qn2t[:, 0:nred], red_ap, axis=AXX))
                    if tch == 0:
                        S.do("vector", lambda e, accd=accd, a0=a0, nred=nred:
                             e.tensor_copy(accd[:, a0:a0 + nred], qn2t[:, 0:nred]),
                             inc="vp")
                    else:
                        S.do("vector", lambda e, accd=accd, a0=a0, nred=nred:
                             e.tensor_tensor(accd[:, a0:a0 + nred],
                                             accd[:, a0:a0 + nred],
                                             qn2t[:, 0:nred], ALU.max), inc="vp")
                    nblk = width // 128
                    dst_fm = Q_fm if proj == "q" else K_fm
                    dch = 8 if proj == "q" else 2
                    c0 = ew * (width // 128) if proj == "q" else 0
                    ev, tl = transpose_tile(
                        lambda cc, b=b: prp[:, b, cc * 128:(cc + 1) * 128],
                        lambda hg, nb, tch=tch, dst_fm=dst_fm, dch=dch, c0=c0:
                        AP(dst_fm, (c0 + hg * 8) * T + tch * 128,
                           [[dch * T, 128], [T, nb], [1, 128]]),
                        nblk, S.now("vp"))
                    prp_rot.set(tch, tl)
                else:
                    v_e = S.do("vector", lambda e, tch=tch, ps_off=ps_off:
                               e.tensor_copy(
                                   AP(V_st, tch * 260,
                                      [[NCHUNK * 260, 128], [65, 4], [1, 64]]),
                                   AP(psF, ps_off, [[3072, 128], [64, 4], [1, 64]])),
                               waits=[last_mm_slot, v_ones], inc="vp")
                    pj_rot.set(idx, v_e)
            wbf_rot.set(slot, last_mm_slot)
            pidx += 1
    v_p2 = S.now("vp")
    kn2_perm = AP(kn2acc, 0, [[32, 128], [8, 2], [0, 4], [4, 2]])
    v_ct0 = cterm_pipe(qn2acc, 0, kn2_perm, cterm0, 0, 16, [v_p2])
    bar("bar2")
    tap("Q_fm", Q_fm)
    tap("K_fm", K_fm)
    tap("V_st", V_st)
    tap("cterm0", cterm0)

    sbuf_free("xn_fm", "wld", "wbf", "ptm", "prp", "rsc", "sqv", "qn2t")
    b_p2 = S.bnd()
    g0 = sbuf("g0", [128, OCHUNK, D], F32)
    w8ld = sbuf("w8ld", [128, 2, 512], FP8)
    w8bf = sbuf("w8bf", [128, 2, DCH, 512], BF16)
    w8bf_rot = Rot(2)
    w8rot = Rot(2)
    ps3_rot = Rot(2)

    s_g0 = None
    for ow in range(4):
        slot = ow % 2
        w8c = {}
        for kc in range(DCH):
            tok = S.dma("sync", w8ld[:, kc % 2, :],
                        l0w8[kc * 128:(kc + 1) * 128, ow * 512:(ow + 1) * 512],
                        waits=[w8rot.get(kc),
                               b_p2[1] if ow == 0 and kc == 0 else None],
                        q=f"qs{kc % 2}")
            w8c[kc] = S.do("vector", lambda e, kc=kc, slot=slot: e.tensor_copy(
                w8bf[:, slot, kc, :], w8ld[:, kc % 2, :]),
                waits=[tok, w8bf_rot.get(slot) if kc == 0 else None] +
                      (b_p2 if ow == 0 and kc == 0 else []), inc="vp")
            w8rot.set(kc, w8c[kc])
        last_mm = None
        for tch in range(OCHUNK):
            idx = ow * OCHUNK + tch
            ps_off = (idx % 2) * 512
            ps = psF[:, ps_off:ps_off + 512]
            for kc in range(DCH):
                last_mm = S.do("tensor", lambda e, kc=kc, tch=tch, slot=slot, ps=ps:
                               e.matmul(ps, xq0_fm[:, kc, tch * 128:(tch + 1) * 128],
                                        w8bf[:, slot, kc, :],
                                        start=(kc == 0), stop=(kc == DCH - 1)),
                               waits=[w8c[kc]] +
                                     ([ps3_rot.get(idx), v_p1] if kc == 0 else []),
                               inc="tp" if kc == DCH - 1 else None)
            s_g0 = S.do("scalar", lambda e, tch=tch, ow=ow, ps=ps: e.activation(
                g0[:, tch, ow * 512:(ow + 1) * 512], ps, AF.Silu,
                scale=sc_misc[:, 1:2]), waits=[last_mm], inc="sp")
            ps3_rot.set(idx, s_g0)
        w8bf_rot.set(slot, last_mm)
    bar("bar3")
    tap("g0", g0)
    tap("w8bf", w8bf)
    if "stop3" in taps:
        zed = sbuf("zed", [128, 512], F32)
        vz = S.do("vector", lambda e: e.memset(zed[:], 0.0))
        for tch in range(OCHUNK):
            for ow in range(4):
                S.dma("gpsimd", out[tch * 128:(tch + 1) * 128,
                                    ow * 512:(ow + 1) * 512], zed[:],
                      waits=[vz, S.now("sp")], q="qg")
        gfin = S.cnt["qg"]
        S.do("gpsimd", lambda e: e.nop(), waits=[("qg", gfin)])
        with nc.Block() as block:
            S.finalize(block)
        return nc, S

    sbuf_free("w8ld", "w8bf", "xq0_fm")
    b_p3 = S.bnd()
    ctx_q = sbuf("ctx_q", [128, NCHUNK, 1024], BF16)
    Pt = sbuf("Pt", [128, 2, 1024], BF16)
    rcp4 = sbuf("rcp4", [128, 8], F32)

    PERMQ = [0, 4, 1, 5, 2, 6, 3, 7, 8, 12, 9, 13, 10, 14, 11, 15]
    kv_of = lambda s: 2 * (s // 8) + (s % 2)
    attention(
        nheads=16, nqw=4, nkc=lambda qw: 4 * qw + 4,
        lhsK=lambda s, kc: K_fm[(s % 2) * 64:(s % 2) * 64 + 64,
                                s // 8, kc * 128:(kc + 1) * 128],
        rhsQ=lambda s, qw: Q_fm[(s % 2) * 64:(s % 2) * 64 + 64, s // 2,
                                qw * 512:(qw + 1) * 512],
        rhsV=lambda s, kc: V_st[:, kc, kv_of(s) * 65:kv_of(s) * 65 + 65],
        biasAP=lambda s: cterm0[:, s:s + 1],
        mask_list=lambda qw, kc: ([("tt", wmk, (kc - 4 * qw) * 512)]
                                  if 0 <= kc - 4 * qw < 4 else []),
        evac_ap=lambda s, qw: AP(ctx_q, 4 * qw * 1024 + PERMQ[s] * 64,
                                 [[NCHUNK * 1024, 128], [1024, 4], [1, 64]]),
        first_wait=[v_ct0, s_g0], PtT=Pt, rcpT=rcp4)
    bar("bar4")
    tap("ctx_q", ctx_q)

    sbuf_free("Q_fm", "K_fm", "V_st", "Pt", "rcp4")
    b_p4 = S.bnd()
    x1 = sbuf("x1", [128, OCHUNK, D], F32)
    wld2 = sbuf("wld2", [128, 2, 256], F32)
    wbf2 = sbuf("wbf2", [128, 2, DCH, 256], BF16)
    x0ld = sbuf("x0ld", [128, 2, 256], F32)
    wld2_rot = Rot(2)
    wbf2_rot = Rot(2)
    x0_rot = Rot(2)
    ps5_rot = Rot(2)

    for ow in range(8):
        slot = ow % 2
        w2c = {}
        for kc in range(DCH):
            tok = S.dma("sync", wld2[:, kc % 2, :],
                        l0wo[kc * 128:(kc + 1) * 128, ow * 256:(ow + 1) * 256],
                        waits=[wld2_rot.get(kc),
                               b_p4[1] if ow == 0 and kc == 0 else None],
                        q=f"qs{2 + kc % 2}")
            w2c[kc] = S.do("scalar", lambda e, kc=kc, slot=slot: e.copy(
                wbf2[:, slot, kc, :], wld2[:, kc % 2, :]),
                waits=[tok, wbf2_rot.get(slot) if kc == 0 else None] +
                      (b_p4 if ow == 0 and kc == 0 else []), inc="sp")
            wld2_rot.set(kc, w2c[kc])
        last_mm = None
        for pch in range(OCHUNK):
            idx = ow * OCHUNK + pch
            x0tok = S.dma("gpsimd", x0ld[:, pch % 2, :],
                          x_own[pch * 128:(pch + 1) * 128, ow * 256:(ow + 1) * 256],
                          waits=[x0_rot.get(idx),
                                 b_p4[0] if idx == 0 else None],
                          q=f"qg{1 + pch % 2}")
            ps_off = (idx % 2) * 512
            ps = psF[:, ps_off:ps_off + 256]
            for kq in range(DCH):
                last_mm = S.do("tensor", lambda e, kq=kq, pch=pch, slot=slot, ps=ps:
                               e.matmul(ps, ctx_q[:, kq, pch * 128:(pch + 1) * 128],
                                        wbf2[:, slot, kq, :],
                                        start=(kq == 0), stop=(kq == DCH - 1)),
                               waits=[w2c[kq]] +
                                     ([ps5_rot.get(idx)] if kq == 0 else []),
                               inc="tp" if kq == DCH - 1 else None)
            b = pch % 2
            S.do("vector", lambda e, b=b, pch=pch, ow=ow, ps=ps: e.tensor_tensor(
                x1[:, pch, ow * 256:(ow + 1) * 256], ps,
                g0[:, pch, ow * 256:(ow + 1) * 256], ALU.mult),
                waits=[last_mm, s_g0, x0tok])
            v_x1t = S.do("vector", lambda e, b=b, pch=pch, ow=ow: e.tensor_tensor(
                x1[:, pch, ow * 256:(ow + 1) * 256],
                x1[:, pch, ow * 256:(ow + 1) * 256], x0ld[:, b, :],
                ALU.add), inc="vp")
            ps5_rot.set(idx, v_x1t)
            x0_rot.set(idx, v_x1t)
        wbf2_rot.set(slot, last_mm)
    v_x1 = S.now("vp")
    bar("bar5")
    tap("x1", x1)

    sbuf_free("ctx_q", "wld2", "wbf2", "x0ld", "g0")
    b_p5 = S.bnd()
    xn1_fm = sbuf("xn1_fm", [128, 17, TOWN], BF16)
    xq1_fm = sbuf("xq1_fm", [128, DCH, TOWN], BF16)
    xn_tm6 = sbuf("xn_tm6", [128, 4, D], F32)
    xnb6 = sbuf("xnb6", [128, 2, D], BF16)
    xi8 = sbuf("xi8", [128, 2, D], I8)
    xq1b = sbuf("xq1b", [128, 2, D], BF16)

    v_onesrow = S.do("vector", lambda e: e.memset(xn1_fm[:, 16, :], 1.0),
                     waits=b_p5, inc="vp")
    xn6_rot = Rot(4)
    xn62_rot = Rot(4)
    xnb6_rot = Rot(2)
    xi8_rot = Rot(2)
    xq1b_rot = Rot(2)
    spill_toks = {}

    def p6_stats(tch):
        return ln_stats(x1[:, tch, :], [v_x1] + (b_p5 if tch == 0 else []),
                        tch % 2)

    def p6_consume(tch, v_st):
        b = tch % 4
        sl = tch % 2
        src = x1[:, tch, :]
        s_xn = S.do("scalar", lambda e, b=b, src=src, sl=sl: e.activation(
            xn_tm6[:, b, :], src, AF.Identity, scale=statT[:, sl, 27:28],
            bias=statT[:, sl, 28:29]),
            waits=[v_st, xn6_rot.get(tch), xn62_rot.get(tch)], inc="sp")
        spill_toks[tch] = S.dma("gpsimd", x1_scr[tch * 128:(tch + 1) * 128, :],
                                src, waits=[v_st], q="qg3")
        v_cast = S.do("vector", lambda e, b=b, b2=sl: e.tensor_copy(
            xnb6[:, b2, :], xn_tm6[:, b, :]), waits=[s_xn, xnb6_rot.get(tch)],
            inc="vp")
        xn6_rot.set(tch, v_cast)
        ev, tl = transpose_tile(
            lambda cc, b2=sl: xnb6[:, b2, cc * 128:(cc + 1) * 128],
            lambda hg, nb, tch=tch: AP(xn1_fm, hg * 8 * TOWN + tch * 128,
                                       [[17 * TOWN, 128], [TOWN, nb], [1, 128]]),
            DCH, v_cast)
        xnb6_rot.set(tch, tl)
        S.do("vector", lambda e, b=b, sl=sl: e.reduce_max(
            statT[:, sl, 29:30], xn_tm6[:, b, :], axis=AXX,
            apply_absolute_value=True), waits=[s_xn])
        S.do("vector", lambda e, sl=sl: e.tensor_scalar(
            statT[:, sl, 30:31], statT[:, sl, 29:30], 1e-5, None, ALU.max))
        S.do("vector", lambda e, sl=sl: e.reciprocal(
            statT[:, sl, 31:32], statT[:, sl, 30:31]))
        S.do("vector", lambda e, sl=sl: e.tensor_scalar(
            statT[:, sl, 32:33], statT[:, sl, 31:32], 127.0, None, ALU.mult))
        v_sc = S.do("vector", lambda e, tch=tch, sl=sl: e.tensor_scalar(
            rall[:, tch:tch + 1], statT[:, sl, 30:31], 1.0 / 127.0, None,
            ALU.mult), inc="vp")
        s_i8 = S.do("scalar", lambda e, b=b, b2=sl, sl=sl: e.activation(
            xi8[:, b2, :], xn_tm6[:, b, :], AF.Identity,
            scale=statT[:, sl, 32:33]),
            waits=[v_sc, xi8_rot.get(tch)], inc="sp")
        xn62_rot.set(tch, s_i8)
        ln_guard.set(tch, s_i8)
        v_bf = S.do("vector", lambda e, b2=sl: e.tensor_copy(
            xq1b[:, b2, :], xi8[:, b2, :]),
            waits=[s_i8, xq1b_rot.get(tch)], inc="vp")
        xi8_rot.set(tch, v_bf)
        ev, tl = transpose_tile(
            lambda cc, b2=sl: xq1b[:, b2, cc * 128:(cc + 1) * 128],
            lambda hg, nb, tch=tch: AP(xq1_fm, hg * 8 * TOWN + tch * 128,
                                       [[DCH * TOWN, 128], [TOWN, nb], [1, 128]]),
            DCH, v_bf)
        xq1b_rot.set(tch, tl)

    vst_p6 = p6_stats(0)
    for tch in range(OCHUNK):
        nxt = p6_stats(tch + 1) if tch + 1 < OCHUNK else None
        p6_consume(tch, vst_p6)
        vst_p6 = nxt
    tap("rall", rall)
    S.do("vector", lambda e: e.tensor_tensor(
        sc1a[:], rall[:], AP(mu128, 4, [[8, 128], [0, 8]]), ALU.mult),
        waits=[v_mu128, S.now("vp")])
    v_p6 = S.do("vector", lambda e: e.tensor_tensor(
        sc3a[:], rall[:], AP(mu128, 5, [[8, 128], [0, 8]]), ALU.mult), inc="vp")
    bar("bar6")
    tap("xn1_fm", xn1_fm)
    tap("xq1_fm", xq1_fm)

    sbuf_free("xn_tm6", "xnb6", "xi8", "xq1b", "x1")
    b_p6 = S.bnd()
    spill_all = spill_toks[OCHUNK - 1]
    Q_fm1 = sbuf("Q_fm1", [128, 16, TOWN], BF16)
    wld3 = sbuf("wld3", [128, 2, 512], F32)
    wbf3 = sbuf("wbf3", [128, 2, 17, 512], BF16)
    ptm3 = sbuf("ptm3", [128, 2, 512], BF16)
    sqv3 = sbuf("sqv3", [128, 512], BF16)
    qn2t3 = sbuf("qn2t3", [128, 8], F32)
    wld3_rot = Rot(2)
    wbf3_rot = Rot(2)
    pj3_rot = Rot(2)
    ptm3_rot = Rot(2)
    ag_fin = [None, None]

    pidx3 = 0
    for proj, wsrc in (("q", wqtI), ("k", wktI), ("v", wvtI)):
        for ew in range(4):
            slot = pidx3 % 2
            w3c = {}
            for kc in range(17):
                nr = 128 if kc < 16 else 1
                tok = S.dma("sync", wld3[0:nr, kc % 2, :],
                            wsrc[kc * 128:kc * 128 + nr, ew * 512:(ew + 1) * 512],
                            waits=[wld3_rot.get(kc), spill_all,
                                   b_p6[1] if pidx3 == 0 and kc == 0 else None],
                            q=f"qs{4 + kc % 2}")
                w3c[kc] = S.do("scalar", lambda e, kc=kc, nr=nr, slot=slot:
                               e.copy(wbf3[0:nr, slot, kc, :],
                                      wld3[0:nr, kc % 2, :]),
                               waits=[tok, wbf3_rot.get(slot) if kc == 0 else None] +
                                     (b_p6 + [spill_all] if pidx3 == 0 and kc == 0
                                      else []), inc="sp")
                wld3_rot.set(kc, w3c[kc])
            last_mm = None
            for tch in range(OCHUNK):
                idx = pidx3 * OCHUNK + tch
                ps_off = (idx % 2) * 512
                ps = psF[:, ps_off:ps_off + 512]
                for kc in range(17):
                    lhs = (xn1_fm[:, kc, tch * 128:(tch + 1) * 128] if kc < 16
                           else xn1_fm[0:1, 16, tch * 128:(tch + 1) * 128])
                    rhs = (wbf3[:, slot, kc, :] if kc < 16
                           else wbf3[0:1, slot, 16, :])
                    last_mm = S.do("tensor", lambda e, lhs=lhs, rhs=rhs, kc=kc,
                                   ps=ps: e.matmul(ps, lhs, rhs, start=(kc == 0),
                                                   stop=(kc == 16)),
                                   waits=[w3c[kc]] +
                                         ([pj3_rot.get(idx), v_onesrow,
                                           v_p6] if kc == 0 else []),
                                   inc="tp" if kc == 16 else None)
                b = tch % 2
                v_e = S.do("vector", lambda e, b=b, ps=ps: e.tensor_copy(
                    ptm3[:, b, :], ps), waits=[last_mm, ptm3_rot.get(tch)],
                    inc="vp")
                pj3_rot.set(idx, v_e)
                if proj == "q":
                    S.do("vector", lambda e, b=b: e.tensor_tensor(
                        sqv3[:], ptm3[:, b, :], ptm3[:, b, :], ALU.mult),
                        waits=[v_e])
                    S.do("vector", lambda e: e.reduce_sum(
                        qn2t3[:, 0:8], AP(sqv3, 0, [[512, 128], [64, 8], [1, 64]]),
                        axis=AXX))
                    if tch == 0:
                        S.do("vector", lambda e, ew=ew: e.tensor_copy(
                            qn2acc[:, ew * 8:ew * 8 + 8], qn2t3[:, 0:8]), inc="vp")
                    else:
                        S.do("vector", lambda e, ew=ew: e.tensor_tensor(
                            qn2acc[:, ew * 8:ew * 8 + 8],
                            qn2acc[:, ew * 8:ew * 8 + 8], qn2t3[:, 0:8], ALU.max),
                            inc="vp")
                    ev, tl = transpose_tile(
                        lambda cc, b=b: ptm3[:, b, cc * 128:(cc + 1) * 128],
                        lambda hg, nb, tch=tch, ew=ew: AP(
                            Q_fm1, ew * 4 * TOWN + tch * 128,
                            [[16 * TOWN, 128], [TOWN, nb], [1, 128]]),
                        4, S.now("vp"))
                    ptm3_rot.set(tch, tl)
                else:
                    col0 = (0 if proj == "k" else D) + ew * 512
                    gt = S.dma("gpsimd", ag_in[tch * 128:(tch + 1) * 128,
                                               col0:col0 + 512],
                               ptm3[:, b, :], waits=[v_e], q=f"qg{4 + b}")
                    ag_fin[b] = gt
                    ptm3_rot.set(tch, gt)
            wbf3_rot.set(slot, last_mm)
            pidx3 += 1
    cc_ag = S.do("gpsimd", lambda e: e.collective_compute(
        "AllGather", ALU.bypass, replica_groups=[[0, 1], [2, 3], [4, 5], [6, 7]],
        ins=[ag_in.ap().opt()], outs=[ag_out.ap().opt()]),
        waits=ag_fin, inc="cc")
    bar("bar7")
    tap("Q_fm1", Q_fm1)

    sbuf_free("wld3", "wbf3", "ptm3", "sqv3", "qn2t3", "xn1_fm")
    b_p7 = S.bnd()
    w13ld = sbuf("w13ld", [128, 4, 512], F32)
    t8b = sbuf("t8b", [128, 4, 512], I8)
    w13bf = sbuf("w13bf", [128, 2, 2, DCH, 512], BF16)
    hbuf = sbuf("hbuf", [128, 2, 512], F32)
    t1b = sbuf("t1b", [128, 2, 512], F32)
    w13ld_rot = Rot(4)
    t8_rot = Rot(4)
    w13_rot = Rot(2)
    ps8_rot = Rot(4)
    hb_rot = Rot(2)
    t1_rot = Rot(2)
    h_fin = [None, None]

    def tern8(hw, ci0, ci1):
        slot = hw % 2
        vtok = None
        for ci in range(ci0, ci1):
            which, kc = divmod(ci, DCH)
            wsrc = w1tI if which == 0 else w3tI
            qi = hw * 32 + ci
            tok = S.dma("sync", w13ld[:, qi % 4, :],
                        wsrc[kc * 128:(kc + 1) * 128, hw * 512:(hw + 1) * 512],
                        waits=[w13ld_rot.get(qi),
                               b_p7[1] if qi == 0 else None],
                        q=f"qs{qi % 4}")
            s_t = S.do("scalar", lambda e, b4=qi % 4, mucol=which: e.activation(
                t8b[:, b4, :], w13ld[:, b4, :], AF.Identity,
                scale=mu128[:, mucol:mucol + 1]),
                waits=[tok, t8_rot.get(qi)] +
                      (b_p7 if qi == 0 else []), inc="sp")
            w13ld_rot.set(qi, s_t)
            vtok = S.do("vector", lambda e, b4=qi % 4, slot=slot, which=which,
                        kc=kc: e.tensor_scalar(
                            w13bf[:, slot, which, kc, :], t8b[:, b4, :],
                            -1.0, 1.0, ALU.max, ALU.min),
                        waits=[s_t, w13_rot.get(slot) if qi % 32 == 0 else None],
                        inc="vp")
            t8_rot.set(qi, vtok)
            vt8[(hw, which, kc)] = vtok
        return vtok

    vt8 = {}
    tern8(0, 0, 32)
    for hw in range(11):
        slot = hw % 2
        last1 = last3 = None
        for tch in range(OCHUNK):
            if hw + 1 < 11:
                tern8(hw + 1, 4 * tch, 4 * tch + 4)
            idx = hw * OCHUNK + tch
            o1 = (idx % 2) * 1024
            ps1 = psF[:, o1:o1 + 512]
            ps3 = psF[:, o1 + 512:o1 + 1024]
            for kc in range(DCH):
                last1 = S.do("tensor", lambda e, kc=kc, tch=tch, slot=slot, ps1=ps1:
                             e.matmul(ps1, xq1_fm[:, kc, tch * 128:(tch + 1) * 128],
                                      w13bf[:, slot, 0, kc, :],
                                      start=(kc == 0), stop=(kc == DCH - 1)),
                             waits=[vt8[(hw, 0, kc)]] +
                                   ([ps8_rot.get(2 * idx)] if kc == 0 else []),
                             inc="tp" if kc == DCH - 1 else None)
            for kc in range(DCH):
                last3 = S.do("tensor", lambda e, kc=kc, tch=tch, slot=slot, ps3=ps3:
                             e.matmul(ps3, xq1_fm[:, kc, tch * 128:(tch + 1) * 128],
                                      w13bf[:, slot, 1, kc, :],
                                      start=(kc == 0), stop=(kc == DCH - 1)),
                             waits=[vt8[(hw, 1, kc)]] +
                                   ([ps8_rot.get(2 * idx + 1)] if kc == 0 else []),
                             inc="tp" if kc == DCH - 1 else None)
            b = tch % 2
            s_si = S.do("scalar", lambda e, b=b, tch=tch, ps1=ps1: e.activation(
                t1b[:, b, :], ps1, AF.Silu, scale=sc1a[:, tch:tch + 1]),
                waits=[last1, t1_rot.get(tch)], inc="sp")
            ps8_rot.set(2 * idx, s_si)
            v_t3 = S.do("vector", lambda e, b=b, tch=tch, ps3=ps3: e.tensor_scalar(
                hbuf[:, b, :], ps3, sc3a[:, tch:tch + 1], None, ALU.mult),
                waits=[last3, s_si, hb_rot.get(tch)], inc="vp")
            ps8_rot.set(2 * idx + 1, v_t3)
            S.do("vector", lambda e, b=b: e.tensor_tensor(
                hbuf[:, b, :], t1b[:, b, :], hbuf[:, b, :], ALU.mult))
            S.do("vector", lambda e, b=b: e.reduce_max(
                statT[:, 0, 34:35], hbuf[:, b, :], axis=AXX,
                apply_absolute_value=True))
            if hw == 0:
                v_am = S.do("vector", lambda e, tch=tch: e.tensor_copy(
                    am2[:, tch:tch + 1], statT[:, 0, 34:35]), inc="vp")
            else:
                v_am = S.do("vector", lambda e, tch=tch: e.tensor_tensor(
                    am2[:, tch:tch + 1], am2[:, tch:tch + 1], statT[:, 0, 34:35],
                    ALU.max), inc="vp")
            t1_rot.set(tch, v_am)
            ht = S.dma("gpsimd", h_scr[tch * 128:(tch + 1) * 128,
                                       hw * 512:(hw + 1) * 512],
                       hbuf[:, b, :], waits=[v_am], q=f"qg{6 + b}")
            h_fin[b] = ht
            hb_rot.set(tch, ht)
        w13_rot.set(slot, last3)
    S.do("vector", lambda e: e.tensor_scalar(scq2[:], am2[:], 1e-5, None, ALU.max),
         waits=[S.now("vp")])
    S.do("vector", lambda e: e.tensor_scalar(scg2[:], scq2[:], 1.0 / 127.0, None,
                                             ALU.mult))
    S.do("vector", lambda e: e.tensor_tensor(
        scg2[:], scg2[:], AP(mu128, 6, [[8, 128], [0, 8]]), ALU.mult),
        waits=[v_mu128])
    S.do("vector", lambda e: e.reciprocal(scq2[:], scq2[:]))
    v_scq = S.do("vector", lambda e: e.tensor_scalar(scq2[:], scq2[:], 127.0, None,
                                                     ALU.mult), inc="vp")

    bar("bar8")
    sbuf_free("w13ld", "t8b", "w13bf", "hbuf", "t1b", "xq1_fm")
    b_p8 = S.bnd()
    K_fm1 = sbuf("K_fm1", [128, 8, T], BF16)
    V_st1 = sbuf("V_st1", [128, NCHUNK, 1040], BF16)
    ctx1 = sbuf("ctx1", [128, OCHUNK, D], BF16)
    Pt9 = sbuf("Pt9", [128, 2, 1024], BF16)
    rcp49 = sbuf("rcp49", [128, 8], F32)
    kvld = sbuf("kvld", [128, 2, 1024], BF16)
    vvld = sbuf("vvld", [128, 2, 1024], BF16)
    sqv9 = sbuf("sqv9", [128, 1024], BF16)
    kn2t9 = sbuf("kn2t9", [128, 16], F32)
    kv_rot = Rot(2)
    vv_rot = Rot(2)

    for grp in range(2):
        for kc in range(NCHUNK):
            ktok = S.dma("sync", kvld[:, kc % 2, :],
                         ag_out[kc * 128:(kc + 1) * 128,
                                grp * 1024:grp * 1024 + 1024],
                         waits=[kv_rot.get(kc), cc_ag] + h_fin +
                               [b_p8[1] if grp == 0 and kc == 0 else None],
                         q=f"qs{4 + kc % 2}")
            vtok2 = S.dma("sync", vvld[:, kc % 2, :],
                          ag_out[kc * 128:(kc + 1) * 128,
                                 D + grp * 1024:D + grp * 1024 + 1024],
                          waits=[vv_rot.get(kc), cc_ag] + h_fin,
                          q=f"qs{6 + kc % 2}")
            S.do("vector", lambda e, b=kc % 2: e.tensor_tensor(
                sqv9[:], kvld[:, b, :], kvld[:, b, :], ALU.mult),
                waits=[ktok] + (b_p8 if grp == 0 and kc == 0 else []))
            S.do("vector", lambda e: e.reduce_sum(
                kn2t9[:, 0:16], AP(sqv9, 0, [[1024, 128], [64, 16], [1, 64]]),
                axis=AXX))
            if kc == 0:
                S.do("vector", lambda e, grp=grp: e.tensor_copy(
                    kn2acc[:, grp * 16:grp * 16 + 16], kn2t9[:, 0:16]), inc="vp")
            else:
                S.do("vector", lambda e, grp=grp: e.tensor_tensor(
                    kn2acc[:, grp * 16:grp * 16 + 16],
                    kn2acc[:, grp * 16:grp * 16 + 16], kn2t9[:, 0:16], ALU.max),
                    inc="vp")
            ev, tl = transpose_tile(
                lambda cc, b=kc % 2: kvld[:, b, cc * 128:(cc + 1) * 128],
                lambda hg, nb, kc=kc: AP(K_fm1, hg * 8 * T + kc * 128,
                                         [[8 * T, 128], [T, nb], [1, 128]]),
                8, S.now("vp"))
            kv_rot.set(kc, tl)
            v_v1 = S.do("vector", lambda e, b=kc % 2, kc=kc: e.tensor_scalar(
                AP(V_st1, kc * 1040, [[NCHUNK * 1040, 128], [65, 16], [1, 64]]),
                vvld[:, b, :], alive[:, kc:kc + 1], None, ALU.mult),
                waits=[vtok2, S.now("tp")], inc="vp")
            v_v2 = S.do("vector", lambda e, kc=kc: e.tensor_copy(
                AP(V_st1, kc * 1040 + 64, [[NCHUNK * 1040, 128], [65, 16]]),
                AP(alive, kc, [[16, 128], [0, 16]])), inc="vp")
            vv_rot.set(kc, v_v2)
        v_ct1 = cterm_pipe(qn2acc, grp * 16,
                           AP(kn2acc, grp * 16, [[32, 128], [1, 16]]),
                           cterm1, grp * 16, 16, [S.now("vp")])
        attention(
            nheads=16, nqw=2, nkc=lambda qw: 12 if qw == 0 else NCHUNK,
            lhsK=lambda lh, kc: K_fm1[(lh % 2) * 64:(lh % 2) * 64 + 64, lh // 2,
                                      kc * 128:(kc + 1) * 128],
            rhsQ=lambda lh, qw, grp=grp: Q_fm1[
                ((grp * 16 + lh) % 2) * 64:((grp * 16 + lh) % 2) * 64 + 64,
                (grp * 16 + lh) // 2, qw * 512:(qw + 1) * 512],
            rhsV=lambda lh, kc: V_st1[:, kc, lh * 65:lh * 65 + 65],
            biasAP=lambda lh, grp=grp: cterm1[:, grp * 16 + lh:grp * 16 + lh + 1],
            mask_list=lambda qw, kc: (
                ([("tt", w1mk, (kc - 4 * qw) * 512)]
                 if 0 <= kc - 4 * qw < 4 else []) +
                ([("tt", wmk, (kc - 8 - 4 * qw) * 512)]
                 if 0 <= kc - 8 - 4 * qw < 4 else []) +
                ([("ts", zA[:, 0:1])] if 4 * qw + 4 <= kc < 8 else [])),
            evac_ap=lambda lh, qw, grp=grp: AP(
                ctx1, 4 * qw * D + (grp * 16 + lh) * 64,
                [[OCHUNK * D, 128], [D, 4], [1, 64]]),
            first_wait=[v_ct1], PtT=Pt9, rcpT=rcp49)
    bar("bar9")
    tap("ctx1", ctx1)

    sbuf_free("K_fm1", "V_st1", "Pt9", "rcp49", "kvld", "vvld", "sqv9",
               "kn2t9", "Q_fm1")
    b_p9 = S.bnd()
    ctx1f = sbuf("ctx1f", [128, 17, TOWN], BF16)
    s1 = sbuf("s1", [128, OCHUNK, D], F32)
    wld4 = sbuf("wld4", [128, 2, 512], F32)
    wbf4 = sbuf("wbf4", [128, 2, 17, 512], BF16)
    wld4_rot = Rot(2)
    wbf4_rot = Rot(2)
    ps10_rot = Rot(2)
    v_or = S.do("vector", lambda e: e.memset(ctx1f[0:1, 16, :], 1.0),
                waits=b_p9, inc="vp")
    v_ctx9 = S.now("vp")
    for tch in range(OCHUNK):
        transpose_tile(
            lambda cc, tch=tch: ctx1[:, tch, cc * 128:(cc + 1) * 128],
            lambda hg, nb, tch=tch: AP(ctx1f, hg * 8 * TOWN + tch * 128,
                                       [[17 * TOWN, 128], [TOWN, nb], [1, 128]]),
            DCH, v_ctx9)
    v_ctxf = S.now("vp")

    v_s1d = None
    for ow in range(4):
        slot = ow % 2
        w4c = {}
        for kc in range(17):
            nr = 128 if kc < 16 else 1
            tok = S.dma("sync", wld4[0:nr, kc % 2, :],
                        wotI[kc * 128:kc * 128 + nr, ow * 512:(ow + 1) * 512],
                        waits=[wld4_rot.get(kc),
                               b_p9[1] if ow == 0 and kc == 0 else None],
                        q=f"qs{kc % 2}")
            w4c[kc] = S.do("scalar", lambda e, kc=kc, nr=nr, slot=slot: e.copy(
                wbf4[0:nr, slot, kc, :], wld4[0:nr, kc % 2, :]),
                waits=[tok, wbf4_rot.get(slot) if kc == 0 else None], inc="sp")
            wld4_rot.set(kc, w4c[kc])
        last_mm = None
        for tch in range(OCHUNK):
            idx = ow * OCHUNK + tch
            ps_off = (idx % 2) * 512
            ps = psF[:, ps_off:ps_off + 512]
            for kc in range(17):
                lhs = (ctx1f[:, kc, tch * 128:(tch + 1) * 128] if kc < 16
                       else ctx1f[0:1, 16, tch * 128:(tch + 1) * 128])
                rhs = wbf4[:, slot, kc, :] if kc < 16 else wbf4[0:1, slot, 16, :]
                last_mm = S.do("tensor", lambda e, lhs=lhs, rhs=rhs, kc=kc, ps=ps:
                               e.matmul(ps, lhs, rhs, start=(kc == 0),
                                        stop=(kc == 16)),
                               waits=[w4c[kc]] +
                                     ([ps10_rot.get(idx), v_ctxf,
                                       v_or] if kc == 0 else []),
                               inc="tp" if kc == 16 else None)
            v_s1d = S.do("vector", lambda e, tch=tch, ow=ow, ps=ps: e.tensor_copy(
                s1[:, tch, ow * 512:(ow + 1) * 512], ps), waits=[last_mm], inc="vp")
            ps10_rot.set(idx, v_s1d)
        wbf4_rot.set(slot, last_mm)
    bar("bar10")
    tap("s1", s1)

    sbuf_free("ctx1", "ctx1f", "wld4", "wbf4")
    b_p10 = S.bnd()
    hq_fm = sbuf("hq_fm", [128, HCH, TOWN], BF16)
    hld = sbuf("hld", [128, 2, 512], F32)
    h8 = sbuf("h8", [128, 2, 512], I8)
    hqb = sbuf("hqb", [128, 2, 512], BF16)
    hld_rot = Rot(2)
    h8_rot = Rot(2)
    hqb_rot = Rot(2)

    for tch in range(OCHUNK):
        for hw in range(11):
            idx = tch * 11 + hw
            tok = S.dma("sync", hld[:, idx % 2, :],
                        h_scr[tch * 128:(tch + 1) * 128, hw * 512:(hw + 1) * 512],
                        waits=[hld_rot.get(idx)] + h_fin +
                              [b_p10[1] if idx == 0 else None],
                        q=f"qs{2 + idx % 2}")
            s_8 = S.do("scalar", lambda e, b=idx % 2, tch=tch: e.activation(
                h8[:, b, :], hld[:, b, :], AF.Identity, scale=scq2[:, tch:tch + 1]),
                waits=[tok, h8_rot.get(idx), v_scq] +
                      (b_p10 if idx == 0 else []), inc="sp")
            hld_rot.set(idx, s_8)
            v_b = S.do("vector", lambda e, b=idx % 2: e.tensor_copy(
                hqb[:, b, :], h8[:, b, :]),
                waits=[s_8, hqb_rot.get(idx)] + (b_p10 if idx == 0 else []),
                inc="vp")
            h8_rot.set(idx, v_b)
            ev, tl = transpose_tile(
                lambda cc, b=idx % 2: hqb[:, b, cc * 128:(cc + 1) * 128],
                lambda hg, nb, tch=tch, hw=hw: AP(
                    hq_fm, (hw * 4) * TOWN + tch * 128,
                    [[HCH * TOWN, 128], [TOWN, nb], [1, 128]]),
                4, v_b)
            hqb_rot.set(idx, tl)
    v_hq = S.now("vp")

    psB16cm.__exit__(None, None, None)
    psG = nc.psum_tensor("psG", [128, 1024], F32).__enter__()

    w2ld = sbuf("w2ld", [128, 4, 512], F32)
    w28 = sbuf("w28", [128, 4, 512], I8)
    NT2 = 8
    tern2 = sbuf("tern2", [128, NT2, 512], BF16)
    t1c = sbuf("t1c", [128, 2, 512], F32)
    t2c = sbuf("t2c", [128, 2, 512], F32)
    x1ld = sbuf("x1ld", [128, 2, 512], F32)
    obuf = sbuf("obuf", [128, 2, 512], F32)
    w2ld_rot = Rot(4)
    w28_rot = Rot(4)
    tern_rot = Rot(NT2)
    ps11_rot = Rot(8)
    x1l_rot = Rot(2)
    ob_rot = Rot(2)
    t1c_rot = Rot(2)

    def ps8ap(tch):
        if tch < 6:
            return psF[:, tch * 512:(tch + 1) * 512]
        return psG[:, (tch - 6) * 512:(tch - 5) * 512]

    NGP = 4 * HCH

    def tern11(gi):
        ow, kc = divmod(gi, HCH)
        tok = S.dma("sync", w2ld[:, gi % 4, :],
                    w2tI[kc * 128:(kc + 1) * 128, ow * 512:(ow + 1) * 512],
                    waits=[w2ld_rot.get(gi), v_hq if gi == 0 else None],
                    q=f"qs{4 + gi % 4}")
        s_t = S.do("scalar", lambda e, b4=gi % 4: e.activation(
            w28[:, b4, :], w2ld[:, b4, :], AF.Identity, scale=mu128[:, 2:3]),
            waits=[tok, w28_rot.get(gi)], inc="sp")
        w2ld_rot.set(gi, s_t)
        v_t = S.do("vector", lambda e, b4=gi % 4, b8=gi % NT2: e.tensor_scalar(
            tern2[:, b8, :], w28[:, b4, :], -1.0, 1.0, ALU.max, ALU.min),
            waits=[s_t, tern_rot.get(gi)], inc="vp")
        w28_rot.set(gi, v_t)
        return v_t

    def evac11(ow, t_fin):
        for tch in range(OCHUNK):
            b = tch % 2
            s_si = S.do("scalar", lambda e, b=b, tch=tch: e.activation(
                t1c[:, b, :], ps8ap(tch), AF.Silu, scale=scg2[:, tch:tch + 1]),
                waits=[t_fin, t1c_rot.get(tch)], inc="sp")
            ps11_rot.set(tch, s_si)
            x1t = S.dma("gpsimd", x1ld[:, b, :],
                        x1_scr[tch * 128:(tch + 1) * 128,
                               ow * 512:(ow + 1) * 512],
                        waits=[x1l_rot.get(tch), spill_all,
                               b_p10[1] if (ow == 0 and tch == 0) else None],
                        q=f"qg{1 + b}")
            S.do("vector", lambda e, b=b, tch=tch, ow=ow: e.tensor_tensor(
                t2c[:, b, :], t1c[:, b, :], s1[:, tch, ow * 512:(ow + 1) * 512],
                ALU.mult), waits=[s_si, v_s1d])
            v_o = S.do("vector", lambda e, b=b: e.tensor_tensor(
                obuf[:, b, :], t2c[:, b, :], x1ld[:, b, :], ALU.add),
                waits=[x1t, ob_rot.get(tch)], inc="vp")
            x1l_rot.set(tch, v_o)
            t1c_rot.set(tch, v_o)
            ot = S.dma("gpsimd", out[tch * 128:(tch + 1) * 128,
                                     ow * 512:(ow + 1) * 512],
                       obuf[:, b, :], waits=[v_o], q=f"qg{8 + b}")
            ob_rot.set(tch, ot)

    v_ts = {gi: tern11(gi) for gi in range(NT2)}
    t_fin = None
    for gp in range(NGP):
        ow, kc = divmod(gp, HCH)
        if kc == 0 and ow > 0:
            evac11(ow - 1, t_fin)
        for tch in range(OCHUNK):
            wl = [v_ts[gp]]
            if kc == 0:
                wl.append(ps11_rot.get(tch))
            t_fin = S.do("tensor", lambda e, kc=kc, tch=tch, b8=gp % NT2:
                         e.matmul(ps8ap(tch),
                                  hq_fm[:, kc, tch * 128:(tch + 1) * 128],
                                  tern2[:, b8, :], start=(kc == 0),
                                  stop=(kc == HCH - 1)),
                         waits=wl if (kc == 0 or tch == 0) else (),
                         inc="tp" if tch == OCHUNK - 1 else None)
        tern_rot.set(gp, t_fin)
        if gp + NT2 < NGP:
            v_ts[gp + NT2] = tern11(gp + NT2)
    evac11(3, t_fin)

    S.do("gpsimd", lambda e: e.nop(),
         waits=[(s, S.cnt[s]) for s in QG_ALL if S.cnt[s] > 0])

    with nc.Block() as block:
        S.finalize(block)
    return nc, S


def _stage_core(inputs, core):
    seq, half = core // 2, core % 2
    f32 = np.float32
    d = {}
    x_seq = np.ascontiguousarray(np.asarray(inputs["L0_attn_x"])[seq], dtype=f32)
    d["x"] = x_seq
    d["x_own"] = np.ascontiguousarray(x_seq[half * 1024:(half + 1) * 1024])
    PERMQ = [0, 4, 1, 5, 2, 6, 3, 7, 8, 12, 9, 13, 10, 14, 11, 15]
    wq_half = np.asarray(inputs["L0_attn_Wq"], f32)[:, half * 1024:(half + 1) * 1024]
    wq_blocks = wq_half.reshape(D, 16, 64)
    d["l0wq"] = np.ascontiguousarray(
        wq_blocks[:, PERMQ, :].reshape(D, 1024))
    d["l0wk"] = np.ascontiguousarray(
        np.asarray(inputs["L0_attn_Wk"], f32)[:, half * 256:(half + 1) * 256])
    d["l0wv"] = np.ascontiguousarray(
        np.asarray(inputs["L0_attn_Wv"], f32)[:, half * 256:(half + 1) * 256])
    d["l0wo"] = np.ascontiguousarray(np.asarray(inputs["L0_attn_Wo"], f32))
    w8 = np.asarray(inputs["L0_ffn_weight"])
    d["l0w8"] = np.ascontiguousarray(w8.T).view(ml_dtypes.float8_e4m3)
    d["w0s"] = np.broadcast_to(
        np.asarray(inputs["L0_ffn_weight_scale"], f32).reshape(1, 1),
        (128, 1)).copy()
    d["in_s"] = np.broadcast_to(
        np.asarray(inputs["L0_ffn_input_scale"]).reshape(1, 1), (128, 1)).copy()
    inv = 1.0 / 10000.0 ** (np.arange(32, dtype=f32) * 2.0 / HD)
    ang = np.arange(T, dtype=f32)[:, None] * inv[None, :]
    d["cosP"] = np.ascontiguousarray(
        np.cos(ang).astype(BF).reshape(16, 128, 32).transpose(1, 0, 2)
        .reshape(128, 512))
    d["sinP"] = np.ascontiguousarray(
        np.sin(ang).astype(BF).reshape(16, 128, 32).transpose(1, 0, 2)
        .reshape(128, 512))
    d["identI"] = np.eye(128, dtype=BF)
    kk = np.arange(128)
    d["tri01"] = (kk[:, None] <= kk[None, :]).astype(BF)
    d["m1"] = d["tri01"] if half == 0 else np.ones((128, 128), BF)
    kk2 = np.arange(128)[:, None]
    ql = np.arange(512)[None, :]
    wm = np.concatenate([(ql >= i * 128 + kk2) for i in range(4)],
                        axis=1).astype(BF)
    d["wmask"] = wm
    d["w1mask"] = wm if half == 0 else np.ones((128, 2048), BF)
    d["zA"] = np.full((128, 1), 0.0 if half == 0 else 1.0, np.float32)
    al = np.zeros((T,), f32)
    al[:(half + 1) * 1024] = 1.0
    d["alive"] = np.ascontiguousarray(al.reshape(16, 128).T)
    hs = np.zeros((128, 2), f32)
    hs[:, half] = 1.0
    d["halfsel"] = hs
    for nm, w, b in (("wqt", "L1_attn_Wq", "L1_attn_bq"),
                     ("wkt", "L1_attn_Wk", "L1_attn_bk"),
                     ("wvt", "L1_attn_Wv", "L1_attn_bv"),
                     ("wot", "L1_attn_Wo", "L1_attn_bo")):
        W = np.asarray(inputs[w], f32)
        bb = np.asarray(inputs[b], f32)
        d[nm] = np.ascontiguousarray(np.concatenate([W.T, bb[None, :]], axis=0))
    d["w1t"] = np.ascontiguousarray(np.asarray(inputs["L1_ffn_w1"], f32).T)
    d["w3t"] = np.ascontiguousarray(np.asarray(inputs["L1_ffn_w3"], f32).T)
    d["w2t"] = np.ascontiguousarray(np.asarray(inputs["L1_ffn_w2"], f32).T)
    d["abs1"] = np.ascontiguousarray(d["w1t"][core * 256:(core + 1) * 256])
    d["abs3"] = np.ascontiguousarray(d["w3t"][core * 256:(core + 1) * 256])
    d["abs2"] = np.ascontiguousarray(d["w2t"][core * 704:(core + 1) * 704])
    return d


_NC_CACHE = {}


def kernel(**inputs):
    if "nc" not in _NC_CACHE:
        _NC_CACHE["nc"] = build()[0]
    nc = _NC_CACHE["nc"]
    in_maps = [_stage_core(inputs, c) for c in range(8)]
    res = run_bass_kernel_spmd(nc, in_maps, core_ids=list(range(8)))
    outp = np.zeros((4, T, D), np.float32)
    for c in range(8):
        seq, half = c // 2, c % 2
        outp[seq, half * 1024:(half + 1) * 1024] = np.asarray(res.results[c]["out"])
    return outp
